# version 06
# baseline (speedup 1.0000x reference)
from contextlib import ExitStack

import numpy as np
import ml_dtypes

import concourse.bass as bass
import concourse.mybir as mybir
from concourse.bass_utils import run_bass_kernel_spmd

F32 = mybir.dt.float32
BF16 = mybir.dt.bfloat16
U8 = mybir.dt.uint8
AF = mybir.ActivationFunctionType
ALU = mybir.AluOpType
AX = mybir.AxisListType

D = 4096
S = 2048
NH = 16
HD = 256
KD = D // 128
MYT = 1024
EPS = 1e-5
NEGV = -1e9
SCL = 1.0 / 16.0

BFNP = np.dtype(ml_dtypes.bfloat16)


def _split512(n):
    out = []
    while n > 0:
        c = min(n, 512)
        out.append(c)
        n -= c
    return out


class Sched:

    def __init__(self):
        self.mode = "count"
        self.cnt = {}
        self.ms = {}
        self.eng = None
        self.engname = None
        self.sems = None
        self.waited = None

    def begin_count(self):
        self.mode = "count"
        self.cnt = {}
        self.ms = {}

    def begin_emit(self, engname, eng, sems):
        self.mode = "emit"
        self.engname = engname
        self.eng = eng
        self.sems = sems
        self.waited = {}

    def op(self, engname, fn, waits=(), inc=None, ms=None):
        if self.mode == "count":
            for k in waits:
                assert k in self.ms, f"wait on undefined milestone {k}"
            if inc is not None:
                s, n = inc
                self.cnt[s] = self.cnt.get(s, 0) + n
                if ms is not None:
                    for k in (ms if isinstance(ms, list) else [ms]):
                        assert k not in self.ms, f"dup milestone {k}"
                        self.ms[k] = (s, self.cnt[s])
            else:
                assert ms is None, ms
            return
        if engname != self.engname:
            return
        for k in waits:
            s, v = self.ms[k]
            if v > self.waited.get(s, 0):
                self.eng.wait_ge(self.sems[s], v)
                self.waited[s] = v
        instr = fn(self.eng)
        if inc is not None:
            s, n = inc
            instr.then_inc(self.sems[s], n)

    def final_wait(self, engname, semname):
        if self.mode == "count" or engname != self.engname:
            return
        v = self.cnt.get(semname, 0)
        if v > self.waited.get(semname, 0):
            self.eng.wait_ge(self.sems[semname], v)
            self.waited[semname] = v


def build_nc(taps=()):
    nc = bass.Bass("TRN2", target_bir_lowering=False)

    xT_d = nc.dram_tensor("xT", [D, S], F32, kind="ExternalInput").ap()
    xres_d = nc.dram_tensor("xres", [MYT, D], F32, kind="ExternalInput").ap()
    wq_d = nc.dram_tensor("wq", [D, D], F32, kind="ExternalInput").ap()
    wk_d = nc.dram_tensor("wk", [D, D], F32, kind="ExternalInput").ap()
    wv_d = nc.dram_tensor("wv", [D, D], F32, kind="ExternalInput").ap()
    wo_d = nc.dram_tensor("wo", [D, D], F32, kind="ExternalInput").ap()
    fw_d = nc.dram_tensor("fw", [D, D], U8, kind="ExternalInput").ap()
    fs_d = nc.dram_tensor("fs", [1, 1], F32, kind="ExternalInput").ap()
    tabq_d = nc.dram_tensor("tabq", [128, MYT], BF16, kind="ExternalInput").ap()
    tabk_d = nc.dram_tensor("tabk", [128, S], BF16, kind="ExternalInput").ap()
    maska_d = nc.dram_tensor("maska", [128, 128], BF16, kind="ExternalInput").ap()
    maskb_d = nc.dram_tensor("maskb", [128, 128], BF16, kind="ExternalInput").ap()
    ident_d = nc.dram_tensor("ident", [128, 128], BF16, kind="ExternalInput").ap()
    out_d = nc.dram_tensor("out", [MYT, D], F32, kind="ExternalOutput").ap()

    qspill = nc.dram_tensor("qspill", [NH, HD, MYT], BF16, kind="Internal").ap()
    kspill = nc.dram_tensor("kspill", [NH, HD, S], BF16, kind="Internal").ap()
    vspill = nc.dram_tensor("vspill", [NH, 128, 16 * HD], BF16, kind="Internal").ap()
    z_hbm = nc.dram_tensor("z_hbm", [8, 128, D], F32, kind="Internal").ap()
    x2_hbm = nc.dram_tensor("x2_hbm", [8, 128, D], F32, kind="Internal").ap()
    z2_hbm = nc.dram_tensor("z2_hbm", [8, 128, D], F32, kind="Internal").ap()

    tap_outs = {}
    if "qkv" in taps:
        tap_outs["tap_q"] = nc.dram_tensor("tap_q", [NH, HD, MYT], BF16, kind="ExternalOutput").ap()
        tap_outs["tap_k"] = nc.dram_tensor("tap_k", [NH, HD, S], BF16, kind="ExternalOutput").ap()
        tap_outs["tap_v"] = nc.dram_tensor("tap_v", [NH, 128, 16 * HD], BF16, kind="ExternalOutput").ap()
    if "cat" in taps:
        tap_outs["tap_cat"] = nc.dram_tensor("tap_cat", [128, 32768], BF16, kind="ExternalOutput").ap()
    if "z" in taps:
        tap_outs["tap_z"] = nc.dram_tensor("tap_z", [8, 128, D], F32, kind="ExternalOutput").ap()
    if "x2" in taps:
        tap_outs["tap_x2"] = nc.dram_tensor("tap_x2", [8, 128, D], F32, kind="ExternalOutput").ap()
    if "z2" in taps:
        tap_outs["tap_z2"] = nc.dram_tensor("tap_z2", [8, 128, D], F32, kind="ExternalOutput").ap()

    es = ExitStack()
    A = es.enter_context(nc.sbuf_tensor("arenaA", [128, 65536], BF16))
    Bu = es.enter_context(nc.sbuf_tensor("arenaB", [128, 65536], U8))
    Bf = Bu.bitcast(F32)
    Bb = Bu.bitcast(BF16)
    identb = es.enter_context(nc.sbuf_tensor("identb", [128, 128], BF16))
    maska = es.enter_context(nc.sbuf_tensor("maska_s", [128, 128], BF16))
    maskb = es.enter_context(nc.sbuf_tensor("maskb_s", [128, 128], BF16))
    tabqc = es.enter_context(nc.sbuf_tensor("tabqc", [64, MYT], BF16))
    tabqs = es.enter_context(nc.sbuf_tensor("tabqs", [64, MYT], BF16))
    tabkc = es.enter_context(nc.sbuf_tensor("tabkc", [64, S], BF16))
    tabks = es.enter_context(nc.sbuf_tensor("tabks", [64, S], BF16))
    stats = es.enter_context(nc.sbuf_tensor("stats", [128, 512], F32))
    ones1 = es.enter_context(nc.sbuf_tensor("ones1", [1, 128], F32))

    ST_BN = 0
    ST_MV = 384
    ST_RSTD = 400
    ST_NMR = 408
    ST_RS = 416
    ST_NB = 424
    ST_FS = 432
    ST_FSB = 433
    ST_FS2 = 434
    ST_NEGFS = 435
    ST_EPS = 436
    ST_RSP = 440

    PL = es.enter_context(nc.psum_tensor("PL", [128, 2048], F32))
    pb = [es.enter_context(nc.psum_tensor(f"pb{i}", [128, 512], F32)) for i in range(4)]
    trP = [p.bitcast(BF16) for p in pb[:2]]
    avP = [pb[2], pb[3]]

    def bank(i):
        if i < 4:
            return PL[:, i * 512:(i + 1) * 512]
        return pb[i - 4][:, :]

    CAT0 = 0
    QKVB = 32768
    SCR = QKVB + 2 * 10240
    AU0 = SCR
    AN0 = SCR + 2048
    AT0 = SCR + 4096
    X2T0 = 32768

    XST, WQST, WKST, WVST = 0, 1024, 2048, 2560
    WQBF, WKBF, WVBF = 7168, 8192, 8704
    QSTG, KSTG, VSTG = 9728, 13824, 22016
    RPQ, RPK, SCQ, SCK = 0, 1024, 3072, 5120
    SV = 4608
    WOST, ZST = 0, 1024
    WOBF = 6144
    ZSL, XSL, TSL, X2F = 0, 2048, 4096, 5120
    X2BF = 12544
    FWST = 0
    FWBF = 512
    Z2ST = 1024
    OST = 5120

    E = Sched()

    def walk():
        QK = {"sync": 12, "gpsimd": 8}
        qc = {"sync": 0, "gpsimd": 0}

        def dma(q, out, in_, waits=(), ms=None):
            j = qc[q]
            qc[q] = j + 1
            sem = f"{q[0]}{j % QK[q]}"
            w = list(waits)
            if j >= QK[q]:
                w.append(("_dma", q, j - QK[q]))
            mss = [("_dma", q, j)] + ([ms] if ms is not None else [])
            E.op(q, lambda e, o=out, i=in_: e.dma_start(out=o, in_=i),
                 w, inc=(sem, 16), ms=mss)

        def op(eng, fn, waits=(), ms=None):
            semname = {"tensor": "t", "vector": "v", "scalar": "s", "gpsimd": "p"}[eng]
            E.op(eng, fn, waits, inc=(semname, 1) if ms is not None else None, ms=ms)

        dma("sync", identb[:, :], ident_d, ms=("d_ident",))
        dma("sync", maska[:, :], maska_d, ms=("d_maska",))
        dma("sync", maskb[:, :], maskb_d, ms=("d_maskb",))
        dma("sync", tabqc[:, :], tabq_d[0:64, :], ms=("tabq",))
        dma("sync", tabqs[:, :], tabq_d[64:128, :], ms=("tabqs",))
        dma("sync", tabkc[:, :], tabk_d[0:64, :], ms=("tabk",))
        dma("sync", tabks[:, :], tabk_d[64:128, :], ms=("tabks",))
        dma("sync", stats[0:1, ST_FS:ST_FS + 1], fs_d, ms=("d_fs",))
        op("vector", lambda e: e.memset(ones1[:, :], 1.0), ms=("ones",))
        op("vector", lambda e: e.memset(stats[:, ST_EPS:ST_EPS + 1], float(EPS)), ms=("epsv",))
        op("tensor", lambda e: e.matmul(pb[0][:, 0:1], ones1[0:1, :],
                                        stats[0:1, ST_FS:ST_FS + 1], start=True, stop=True),
           waits=[("ones",), ("d_fs",)], ms=("fsmm",))
        op("vector", lambda e: e.tensor_copy(stats[:, ST_FSB:ST_FSB + 1], pb[0][:, 0:1]),
           waits=[("fsmm",)], ms=("fsb",))
        op("vector", lambda e: e.tensor_scalar_mul(stats[:, ST_FS2:ST_FS2 + 1],
                                                   stats[:, ST_FSB:ST_FSB + 1], 2.0),
           waits=[("fsb",)], ms=("fs2",))
        op("vector", lambda e: e.tensor_scalar_mul(stats[:, ST_NEGFS:ST_NEGFS + 1],
                                                   stats[:, ST_FSB:ST_FSB + 1], -1.0),
           waits=[("fsb",)], ms=("negfs",))

        for ii in range(128):
            kk, qq = ii // 4, ii % 4
            b = ii % 2
            w = [("xcast", ii - 2)] if ii >= 2 else [("tabq",)]
            dma("gpsimd", Bf[:, XST + b * 512:XST + (b + 1) * 512],
                xT_d[kk * 128:(kk + 1) * 128, qq * 512:(qq + 1) * 512],
                waits=w, ms=("xdma", ii))
            op("scalar",
               lambda e, kk=kk, qq=qq, b=b: e.copy(
                   A[:, kk * 2048 + qq * 512:kk * 2048 + (qq + 1) * 512],
                   Bf[:, XST + b * 512:XST + (b + 1) * 512]),
               waits=[("xdma", ii)], ms=("xcast", ii))

        for p in range(8):
            h0 = 2 * p
            for kk in range(KD):
                b = kk % 2
                if kk >= 2:
                    w = [("qcast", p, kk - 2)]
                elif p > 0:
                    w = [("qmm", p - 1, KD - 1)]
                else:
                    w = [("tabk",)]
                dma("sync", Bf[:, WQST + b * 512:WQST + (b + 1) * 512],
                    wq_d[kk * 128:(kk + 1) * 128, h0 * HD:h0 * HD + 512],
                    waits=w, ms=("qwd", p, kk))
                wc = [("qwd", p, kk)]
                if kk >= 2:
                    wc.append(("qmm", p, kk - 2))
                op("vector",
                   lambda e, b=b: e.tensor_copy(Bb[:, WQBF + b * 512:WQBF + (b + 1) * 512],
                                                Bf[:, WQST + b * 512:WQST + (b + 1) * 512]),
                   waits=wc, ms=("qcast", p, kk))
                for m in range(4):
                    for n in range(2):
                        wmm = [("qcast", p, kk), ("xcast", min(kk * 4 + 3, 127))]
                        if kk == 0:
                            wmm.append(("vevb", p - 1, 1, m * 2 + n) if p > 0 else ("fsb",))
                        op("tensor",
                           lambda e, b=b, m=m, n=n, kk=kk: e.matmul(
                               bank(m * 2 + n),
                               Bb[:, WQBF + b * 512 + m * 128:WQBF + b * 512 + (m + 1) * 128],
                               A[:, kk * 2048 + n * 512:kk * 2048 + (n + 1) * 512],
                               start=(kk == 0), stop=(kk == KD - 1)),
                           waits=wmm if kk == 0 else (wmm if (m == 0 and n == 0) else ()),
                           ms=("qmm", p, kk) if (m == 3 and n == 1) else None)
            for m in range(4):
                for n in range(2):
                    w = [("qmm", p, KD - 1)]
                    if p > 0:
                        w += [("qspill", p - 1, 0), ("qspill", p - 1, 1)]
                    op("vector",
                       lambda e, m=m, n=n: e.tensor_scalar_mul(
                           Bb[:, QSTG + m * 1024 + n * 512:QSTG + m * 1024 + (n + 1) * 512],
                           bank(m * 2 + n), SCL),
                       waits=w, ms=("qevb", p, m * 2 + n))
            for hh in range(2):
                dma("gpsimd",
                    qspill[h0 + hh].rearrange("(c p) t -> p c t", p=128),
                    Bb[:, QSTG + hh * 2048:QSTG + (hh + 1) * 2048].rearrange("p (c t) -> p c t", t=1024),
                    waits=[("qevb", p, i) for i in range(8)], ms=("qspill", p, hh))

            for hh in range(2):
                h = h0 + hh
                for kk in range(KD):
                    b = kk % 2
                    if kk >= 2:
                        w = [("kcast", p, hh, kk - 2)]
                    elif hh == 1:
                        w = [("kmm", p, 0, KD - 2 + kk)]
                    elif p > 0:
                        w = [("kmm", p - 1, 1, KD - 2 + kk)]
                    else:
                        w = [("tabk",)]
                    dma("sync", Bf[:, WKST + b * 256:WKST + (b + 1) * 256],
                        wk_d[kk * 128:(kk + 1) * 128, h * HD:(h + 1) * HD],
                        waits=w, ms=("kwd", p, hh, kk))
                    wc = [("kwd", p, hh, kk)]
                    if kk >= 2:
                        wc.append(("kmm", p, hh, kk - 2))
                    op("vector",
                       lambda e, b=b: e.tensor_copy(Bb[:, WKBF + b * 256:WKBF + (b + 1) * 256],
                                                    Bf[:, WKST + b * 256:WKST + (b + 1) * 256]),
                       waits=wc, ms=("kcast", p, hh, kk))
                    for m in range(2):
                        for n in range(4):
                            wmm = [("kcast", p, hh, kk)]
                            if kk == 0:
                                wmm.append(("qevb", p, m * 4 + n) if hh == 0
                                           else ("kevb", p, 0, m * 4 + n))
                            op("tensor",
                               lambda e, b=b, m=m, n=n, kk=kk: e.matmul(
                                   bank(m * 4 + n),
                                   Bb[:, WKBF + b * 256 + m * 128:WKBF + b * 256 + (m + 1) * 128],
                                   A[:, kk * 2048 + n * 512:kk * 2048 + (n + 1) * 512],
                                   start=(kk == 0), stop=(kk == KD - 1)),
                               waits=wmm if kk == 0 else (wmm if (m == 0 and n == 0) else ()),
                               ms=("kmm", p, hh, kk) if (m == 1 and n == 3) else None)
                for m in range(2):
                    for n in range(4):
                        w = [("kmm", p, hh, KD - 1)]
                        if p > 0:
                            w.append(("kspill", p - 1, hh))
                        op("vector",
                           lambda e, hh=hh, m=m, n=n: e.tensor_copy(
                               Bb[:, KSTG + hh * 4096 + m * 2048 + n * 512:
                                  KSTG + hh * 4096 + m * 2048 + (n + 1) * 512],
                               bank(m * 4 + n)),
                           waits=w, ms=("kevb", p, hh, m * 4 + n))
                dma("gpsimd",
                    kspill[h].rearrange("(c p) t -> p c t", p=128),
                    Bb[:, KSTG + hh * 4096:KSTG + (hh + 1) * 4096].rearrange("p (c t) -> p c t", t=2048),
                    waits=[("kevb", p, hh, i) for i in range(8)], ms=("kspill", p, hh))

            for ps in range(2):
                for kk in range(KD):
                    b = kk % 2
                    if kk >= 2:
                        w = [("vcast", p, ps, kk - 2)]
                    elif ps == 1:
                        w = [("vmm", p, 0, KD - 2 + kk)]
                    elif p > 0:
                        w = [("vmm", p - 1, 1, KD - 2 + kk)]
                    else:
                        w = [("tabk",)]
                    dma("sync", Bf[:, WVST + b * 512:WVST + (b + 1) * 512],
                        wv_d[kk * 128:(kk + 1) * 128, h0 * HD:h0 * HD + 512],
                        waits=w, ms=("vwd", p, ps, kk))
                    wc = [("vwd", p, ps, kk)]
                    if kk >= 2:
                        wc.append(("vmm", p, ps, kk - 2))
                    op("vector",
                       lambda e, b=b: e.tensor_copy(Bb[:, WVBF + b * 512:WVBF + (b + 1) * 512],
                                                    Bf[:, WVST + b * 512:WVST + (b + 1) * 512]),
                       waits=wc, ms=("vcast", p, ps, kk))
                    for bidx in range(8):
                        c = ps * 8 + bidx
                        wmm = [("vcast", p, ps, kk)]
                        if kk == 0:
                            wmm.append(("kevb", p, 1, bidx) if ps == 0 else ("vevb", p, 0, bidx))
                        op("tensor",
                           lambda e, b=b, c=c, bidx=bidx, kk=kk: e.matmul(
                               bank(bidx),
                               A[:, kk * 2048 + c * 128:kk * 2048 + (c + 1) * 128],
                               Bb[:, WVBF + b * 512:WVBF + (b + 1) * 512],
                               start=(kk == 0), stop=(kk == KD - 1)),
                           waits=wmm if kk == 0 else (wmm if bidx == 0 else ()),
                           ms=("vmm", p, ps, kk) if bidx == 7 else None)
                for bidx in range(8):
                    c = ps * 8 + bidx
                    w = [("vmm", p, ps, KD - 1)]
                    if p > 0 and ps == 0:
                        w += [("vspill", p - 1, 0), ("vspill", p - 1, 1)]
                    op("vector",
                       lambda e, c=c, bidx=bidx: e.tensor_copy(
                           Bb[:, VSTG + c * 512:VSTG + (c + 1) * 512], bank(bidx)),
                       waits=w, ms=("vevb", p, ps, bidx))
            for hh in range(2):
                h = h0 + hh
                dma("gpsimd",
                    vspill[h].rearrange("p (c d) -> p c d", d=HD),
                    Bb[:, VSTG:VSTG + 16 * 512].rearrange("p (c two) -> p c two", two=512)[:, :, hh * HD:(hh + 1) * HD],
                    waits=[("vevb", p, 0, i) for i in range(8)] + [("vevb", p, 1, i) for i in range(8)],
                    ms=("vspill", p, hh))

        p1done = ("vspill", 7, 1)
        p1all = [("vspill", 7, 0), ("vspill", 7, 1)]

        grp_hist = []

        def emit_head_loads(h):
            bi = h % 2
            qb = QKVB + bi * 10240
            kb = qb + 2048
            vb = qb + 6144
            wl = [("qspill", h // 2, h % 2), ("kspill", h // 2, h % 2),
                  ("vspill", h // 2, h % 2)] + p1all
            if h >= 2:
                wl.append(("avmm", (h - 2) * 8 + 7))
            dma("gpsimd", A[:, qb:qb + 2048].rearrange("p (c t) -> p c t", t=1024),
                qspill[h].rearrange("(c p) t -> p c t", p=128),
                waits=wl, ms=("ldq", h))
            dma("gpsimd", A[:, kb:kb + 4096].rearrange("p (c t) -> p c t", t=2048),
                kspill[h].rearrange("(c p) t -> p c t", p=128),
                waits=wl, ms=("ldk", h))
            dma("gpsimd", A[:, vb:vb + 4096], vspill[h],
                waits=wl, ms=("ldv", h))
            wr = [("ldq", h)] + ([("ropeq", h - 1)] if h >= 1 else [])
            dma("gpsimd", Bb[0:32, 0:1024], A[32:64, qb:qb + 1024], waits=wr, ms=("swq1", h))
            dma("gpsimd", Bb[32:64, 0:1024], A[0:32, qb:qb + 1024], waits=wr, ms=("swq2", h))
            wr = [("ldk", h)] + ([("ropek", h - 1)] if h >= 1 else [])
            dma("gpsimd", Bb[0:32, 1024:3072], A[32:64, kb:kb + 2048], waits=wr, ms=("swk1", h))
            dma("gpsimd", Bb[32:64, 1024:3072], A[0:32, kb:kb + 2048], waits=wr, ms=("swk2", h))
            wv_ = [("swq1", h), ("swq2", h), ("tabq",), ("tabqs",)] + ([("ropeq", h - 1)] if h >= 1 else [])
            op("gpsimd", lambda e, qb=qb: e.tensor_mul(Bb[0:64, 3072:4096],
                                                       A[0:64, qb:qb + 1024], tabqc[:, :]),
               waits=wv_, ms=("rq1", h))
            op("gpsimd", lambda e: e.tensor_mul(Bb[0:64, 4096:5120],
                                                Bb[0:64, 0:1024], tabqs[:, :]),
               waits=wv_, ms=("rq2", h))
            op("gpsimd", lambda e, qb=qb: e.tensor_add(A[0:64, qb:qb + 1024],
                                                       Bb[0:64, 3072:4096],
                                                       Bb[0:64, 4096:5120]),
               waits=[("rq1", h), ("rq2", h)], ms=("ropeq", h))
            wv_ = [("swk1", h), ("swk2", h), ("tabk",), ("tabks",)] + ([("ropek", h - 1)] if h >= 1 else [])
            op("gpsimd", lambda e, kb=kb: e.tensor_mul(Bb[0:64, 5120:7168],
                                                       A[0:64, kb:kb + 2048], tabkc[:, :]),
               waits=wv_, ms=("rk1", h))
            op("gpsimd", lambda e: e.tensor_mul(Bb[0:64, 7168:9216],
                                                Bb[0:64, 1024:3072], tabks[:, :]),
               waits=wv_, ms=("rk2", h))
            op("gpsimd", lambda e, kb=kb: e.tensor_add(A[0:64, kb:kb + 2048],
                                                       Bb[0:64, 5120:7168],
                                                       Bb[0:64, 7168:9216]),
               waits=[("rk1", h), ("rk2", h)], ms=("ropek", h))

        def emit_lg(t, P):
            h, jj = divmod(t, 8)
            bi = h % 2
            qb = QKVB + bi * 10240
            kb = qb + 2048
            klen = jj + 1
            KW = klen * 128
            wlog = [("ropeq", h), ("ropek", h), ("d_maska",), ("d_maskb",), ("d_ident",)]
            wlog.append(("expp", t - 1, P) if t > 0 else p1done)
            first = True
            pbase = P * 1024
            koff = P * 1024
            coff = 0
            chunks = _split512(KW)
            for ci, csz in enumerate(chunks):
                isdiag = (ci == len(chunks) - 1)
                for m in range(2):
                    stop = (m == 1 and not isdiag)
                    op("tensor",
                       lambda e, m=m, coff=coff, csz=csz, pbase=pbase, koff=koff,
                       qb=qb, kb=kb, jj=jj, stop=stop: e.matmul(
                           PL[:, pbase + coff:pbase + coff + csz],
                           A[:, qb + m * 1024 + jj * 128:qb + m * 1024 + (jj + 1) * 128],
                           A[:, kb + m * 2048 + koff + coff:kb + m * 2048 + koff + coff + csz],
                           start=(m == 0), stop=stop),
                       waits=wlog if first else ())
                    first = False
                if isdiag:
                    mk = maska if P == 0 else maskb
                    mend = pbase + KW
                    op("tensor",
                       lambda e, mk=mk, mend=mend: e.matmul(
                           PL[:, mend - 128:mend], identb[:, :], mk[:, :],
                           start=False, stop=True),
                       ms=("lgp", t, P))
                coff += csz

        def emit_mx(t):
            h, jj = divmod(t, 8)
            gp = t % 3
            KW = (jj + 1) * 128
            SVX = 4608 + 16 * gp
            wmx = [("lgp", t, 0)]
            if t >= 3:
                wmx.append(("smin", t - 3))
            op("vector", lambda e, KW=KW, SVX=SVX: e.reduce_max(
                   Bf[:, SVX:SVX + 1], PL[:, 0:KW], axis=AX.X, negate=True),
               waits=wmx, ms=("mxa", t))
            op("vector", lambda e, KW=KW, SVX=SVX: e.reduce_max(
                   Bf[:, SVX + 1:SVX + 2], PL[:, 1024:1024 + KW], axis=AX.X, negate=True),
               waits=[("lgp", t, 1), ("mxa", t)], ms=("mxb", t))

        def emit_smx(t):
            h, jj = divmod(t, 8)
            gp = t % 3
            KW = (jj + 1) * 128
            W = 2 * KW
            SVX = 4608 + 16 * gp
            AUB = 21632 + 2048 * gp
            AN = SCR + 2048 * gp
            wmin = [("mxa", t), ("mxb", t)]
            if t >= 3:
                wmin.append(("expp", t - 3, 1))
            op("vector", lambda e, SVX=SVX: e.tensor_tensor(
                   Bf[:, SVX + 2:SVX + 3], Bf[:, SVX:SVX + 1],
                   Bf[:, SVX + 1:SVX + 2], ALU.min),
               waits=wmin, ms=("smin", t))
            for P in range(2):
                wexp = [("smin", t)]
                if t >= 3:
                    wexp += [("nrm", t - 3), ("sadd", t - 3)]
                op("scalar", lambda e, P=P, KW=KW, SVX=SVX, AUB=AUB: e.activation(
                       Bb[:, AUB + P * KW:AUB + (P + 1) * KW],
                       PL[:, P * 1024:P * 1024 + KW], AF.Exp,
                       bias=Bf[:, SVX + 2:SVX + 3], scale=1.0,
                       accum_out=Bf[:, SVX + 3 + P:SVX + 4 + P]),
                   waits=wexp, ms=("expp", t, P))
            op("vector", lambda e, SVX=SVX: e.tensor_add(
                   Bf[:, SVX + 5:SVX + 6], Bf[:, SVX + 3:SVX + 4],
                   Bf[:, SVX + 4:SVX + 5]),
               waits=[("expp", t, 0), ("expp", t, 1)], ms=("sadd", t))
            op("vector", lambda e, SVX=SVX: e.reciprocal(
                   Bf[:, SVX + 6:SVX + 7], Bf[:, SVX + 5:SVX + 6]),
               waits=[("sadd", t)], ms=("rcp", t))
            wn = [("rcp", t)]
            if t >= 3:
                wn.append(("trl", *divmod(t - 3, 8)))
            op("vector", lambda e, W=W, AUB=AUB, AN=AN, SVX=SVX: e.tensor_scalar_mul(
                   A[:, AN:AN + W], Bb[:, AUB:AUB + W], Bf[:, SVX + 6:SVX + 7]),
               waits=wn, ms=("nrm", t))

        def emit_trav(t):
            h, jj = divmod(t, 8)
            bi = h % 2
            vb = QKVB + bi * 10240 + 6144
            gp = t % 3
            klen = jj + 1
            KK = 2 * klen
            AN = SCR + 2048 * gp
            AT = SCR + 6144 + 2048 * gp
            ngrp = (KK + 3) // 4
            for g in range(ngrp):
                kks = list(range(4 * g, min(4 * g + 4, KK)))
                tb = len(grp_hist) % 2
                wtr = [("nrm", t)]
                if len(grp_hist) >= 2:
                    wtr.append(grp_hist[-2])
                for i, kk in enumerate(kks):
                    last_of_task = (kk == KK - 1)
                    op("tensor",
                       lambda e, tb=tb, kk=kk, i=i, AN=AN: e.transpose(
                           trP[tb][:, i * 128:(i + 1) * 128],
                           A[:, AN + kk * 128:AN + (kk + 1) * 128], identb[:, :]),
                       waits=wtr if i == 0 else (),
                       ms=("trl", h, jj) if last_of_task else ("trx", h, jj, kk))
                wge = [("trl", h, jj) if kks[-1] == KK - 1 else ("trx", h, jj, kks[-1])]
                if t >= 3:
                    wge.append(("avmm", t - 3))
                gkey = ("grpev", t, g)
                op("vector",
                   lambda e, tb=tb, g=g, n=len(kks), AT=AT: e.tensor_copy(
                       A[:, AT + g * 512:AT + g * 512 + n * 128],
                       trP[tb][:, 0:n * 128]),
                   waits=wge, ms=gkey)
                grp_hist.append(gkey)
            for kk in range(KK):
                c = kk if kk < klen else 8 + (kk - klen)
                wav = [("grpev", t, kk // 4)]
                if kk == 0:
                    wav.append(("avev", t - 1) if t > 0 else p1done)
                    wav.append(("ldv", h))
                for dh in range(2):
                    op("tensor",
                       lambda e, dh=dh, c=c, kk=kk, KK=KK, vb=vb, AT=AT: e.matmul(
                           avP[dh][:, 0:128],
                           A[:, vb + c * 256 + dh * 128:vb + c * 256 + (dh + 1) * 128],
                           A[:, AT + kk * 128:AT + (kk + 1) * 128],
                           start=(kk == 0), stop=(kk == KK - 1)),
                       waits=wav if (dh == 0 or kk == 0) else (),
                       ms=("avmm", t) if (kk == KK - 1 and dh == 1) else None)
            for dh in range(2):
                op("scalar",
                   lambda e, dh=dh, h=h, jj=jj: e.copy(
                       A[:, CAT0 + (2 * h + dh) * 1024 + jj * 128:
                          CAT0 + (2 * h + dh) * 1024 + (jj + 1) * 128],
                       avP[dh][:, 0:128]),
                   waits=[("avmm", t)], ms=("avev", t) if dh == 1 else None)

        emit_head_loads(0)
        emit_head_loads(1)
        T2 = NH * 8
        for u in range(T2 + 2):
            if u < T2:
                h, jj = divmod(u, 8)
                if jj == 3 and 2 <= h + 1 < NH:
                    emit_head_loads(h + 1)
                emit_lg(u, 0)
                emit_lg(u, 1)
                emit_mx(u)
            if u - 2 >= 0:
                emit_trav(u - 2)
            if u < T2:
                emit_smx(u)

        p2done = ("avev", NH * 8 - 1)

        XRS = 3584
        for nn in range(8):
            for kk in range(KD):
                b = kk % 2
                if kk >= 2:
                    w = [("wocast", nn, kk - 2)]
                elif nn > 0:
                    w = [("wocast", nn - 1, KD - 2 + kk)]
                else:
                    w = [p2done]
                dma("sync", Bf[:, WOST + b * 512:WOST + (b + 1) * 512],
                    wo_d[kk * 128:(kk + 1) * 128, nn * 512:(nn + 1) * 512],
                    waits=w, ms=("wod", nn, kk))
                wc = [("wod", nn, kk)]
                if kk >= 2:
                    wc.append(("womm", nn, kk - 2))
                elif nn > 0:
                    wc.append(("womm", nn - 1, KD - 2 + kk))
                op("scalar",
                   lambda e, b=b: e.copy(Bb[:, WOBF + b * 512:WOBF + (b + 1) * 512],
                                         Bf[:, WOST + b * 512:WOST + (b + 1) * 512]),
                   waits=wc, ms=("wocast", nn, kk))
                for m in range(8):
                    wmm = [("wocast", nn, kk), p2done]
                    if kk == 0 and nn > 0:
                        wmm.append(("zev", nn - 1, m))
                    op("tensor",
                       lambda e, m=m, b=b, kk=kk: e.matmul(
                           bank(m),
                           A[:, CAT0 + kk * 1024 + m * 128:CAT0 + kk * 1024 + (m + 1) * 128],
                           Bb[:, WOBF + b * 512:WOBF + (b + 1) * 512],
                           start=(kk == 0), stop=(kk == KD - 1)),
                       waits=wmm if kk == 0 else (wmm if m == 0 else ()),
                       ms=("womm", nn, kk) if m == 7 else None)
            for m in range(8):
                zb = m % 4
                xb = m % 4
                wx = [p2done]
                if nn > 0 or m >= 4:
                    pm = (nn, m - 4) if m >= 4 else (nn - 1, m + 4)
                    wx.append(("zev",) + pm)
                dma("sync", Bf[:, XRS + xb * 512:XRS + (xb + 1) * 512],
                    xres_d[m * 128:(m + 1) * 128, nn * 512:(nn + 1) * 512],
                    waits=wx, ms=("xrsd", nn, m))
                w = [("womm", nn, KD - 1), ("xrsd", nn, m)]
                if nn > 0 or m >= 4:
                    pm = (nn, m - 4) if m >= 4 else (nn - 1, m + 4)
                    w.append(("zdma",) + pm)
                op("vector",
                   lambda e, m=m, zb=zb, xb=xb: e.tensor_add(
                       Bf[:, ZST + zb * 512:ZST + (zb + 1) * 512], bank(m),
                       Bf[:, XRS + xb * 512:XRS + (xb + 1) * 512]),
                   waits=w, ms=("zev", nn, m))
                op("vector",
                   lambda e, m=m, nn=nn, zb=zb: e.bn_stats(
                       stats[:, ST_BN + (m * 8 + nn) * 6:ST_BN + (m * 8 + nn + 1) * 6],
                       Bf[:, ZST + zb * 512:ZST + (zb + 1) * 512]),
                   waits=[("zev", nn, m)], ms=("bnst", m, nn))
                dma("sync", z_hbm[m][:, nn * 512:(nn + 1) * 512],
                    Bf[:, ZST + zb * 512:ZST + (zb + 1) * 512],
                    waits=[("bnst", m, nn)], ms=("zdma", nn, m))

        zall_list = [("zdma", 7, i) for i in range(8)]

        TSL5, X2F5, X2B5 = 0, 2048, 6144
        for m in range(8):
            op("vector",
               lambda e, m=m: e.bn_aggr(stats[:, ST_MV + m * 2:ST_MV + (m + 1) * 2],
                                        stats[:, ST_BN + m * 48:ST_BN + (m + 1) * 48]),
               waits=[("bnst", m, n2) for n2 in range(8)], ms=("mv", m))
            op("scalar",
               lambda e, m=m: e.activation(stats[:, ST_RSTD + m:ST_RSTD + m + 1],
                                           stats[:, ST_MV + m * 2 + 1:ST_MV + m * 2 + 2],
                                           AF.Ln, bias=stats[:, ST_EPS:ST_EPS + 1], scale=1.0),
               waits=[("mv", m), ("epsv",)], ms=("lnv", m))
            op("scalar",
               lambda e, m=m: e.activation(stats[:, ST_RSTD + m:ST_RSTD + m + 1],
                                           stats[:, ST_RSTD + m:ST_RSTD + m + 1],
                                           AF.Exp, bias=0.0, scale=-0.5),
               waits=[("lnv", m)], ms=("rstd", m))
            op("vector",
               lambda e, m=m: e.tensor_mul(stats[:, ST_NMR + m:ST_NMR + m + 1],
                                           stats[:, ST_MV + m * 2:ST_MV + m * 2 + 1],
                                           stats[:, ST_RSTD + m:ST_RSTD + m + 1]),
               waits=[("rstd", m)], ms=("nmr", m))
            for nn in range(8):
                b = nn % 4
                w = list(zall_list) + [("zdma", 7, m)]
                if nn >= 4:
                    w.append(("x2n", m, nn - 4))
                elif m > 0:
                    w.append(("x2n", m - 1, nn + 4))
                dma("sync", Bf[:, TSL5 + b * 512:TSL5 + (b + 1) * 512],
                    z_hbm[m][:, nn * 512:(nn + 1) * 512], waits=w, ms=("zrd", m, nn))
                tb = nn % 2
                wn = [("zrd", m, nn), ("nmr", m)]
                if nn >= 2:
                    wn.append(("x2dma", m, nn - 2))
                elif m > 0:
                    wn.append(("x2dma", m - 1, nn + 6))
                op("vector",
                   lambda e, m=m, b=b, tb=tb: e.tensor_scalar(
                       Bf[:, X2F5 + tb * 512:X2F5 + (tb + 1) * 512],
                       Bf[:, TSL5 + b * 512:TSL5 + (b + 1) * 512],
                       stats[:, ST_RSTD + m:ST_RSTD + m + 1],
                       stats[:, ST_NMR + m:ST_NMR + m + 1],
                       ALU.mult, ALU.subtract),
                   waits=wn, ms=("x2n", m, nn))
                dma("sync", x2_hbm[m][:, nn * 512:(nn + 1) * 512],
                    Bf[:, X2F5 + tb * 512:X2F5 + (tb + 1) * 512],
                    waits=[("x2n", m, nn)], ms=("x2dma", m, nn))
                wxc = [("x2n", m, nn)]
                if nn >= 2:
                    wxc.append(("x2t", m, (nn - 2) * 4 + 3))
                elif m > 0:
                    wxc.append(("x2t", m - 1, (nn + 6) * 4 + 3))
                op("vector",
                   lambda e, tb=tb: e.tensor_copy(
                       Bb[:, X2B5 + tb * 512:X2B5 + (tb + 1) * 512],
                       Bf[:, X2F5 + tb * 512:X2F5 + (tb + 1) * 512]),
                   waits=wxc, ms=("x2c", m, nn))
                wrsp = [("x2c", m, nn)]
                if m > 0 and nn == 0:
                    wrsp.append(("rs", m - 1))
                op("vector",
                   lambda e, nn=nn, tb=tb: e.reduce_sum(
                       stats[:, ST_RSP + nn:ST_RSP + nn + 1],
                       Bb[:, X2B5 + tb * 512:X2B5 + (tb + 1) * 512], axis=AX.X),
                   waits=wrsp, ms=("rsp", m, nn))
                pbk = len(grp_hist) % 2
                wtr = [("x2c", m, nn)]
                if len(grp_hist) >= 2:
                    wtr.append(grp_hist[-2])
                for t4 in range(4):
                    kkc = nn * 4 + t4
                    op("tensor",
                       lambda e, pbk=pbk, tb=tb, t4=t4: e.transpose(
                           trP[pbk][:, t4 * 128:(t4 + 1) * 128],
                           Bb[:, X2B5 + tb * 512 + t4 * 128:X2B5 + tb * 512 + (t4 + 1) * 128],
                           identb[:, :]),
                       waits=wtr if t4 == 0 else (), ms=("x2t", m, kkc))
                gkey = ("x2g", m, nn)
                op("vector",
                   lambda e, pbk=pbk, m=m, nn=nn: e.tensor_copy(
                       A.rearrange("p (c t) -> p c t", t=1024)[:, 32 + nn * 4:36 + nn * 4,
                                                              m * 128:(m + 1) * 128],
                       trP[pbk][:, 0:512].rearrange("p (c t) -> p c t", t=128)),
                   waits=[("x2t", m, nn * 4 + 3)], ms=gkey)
                grp_hist.append(gkey)
            op("vector",
               lambda e, m=m: e.reduce_sum(stats[:, ST_RS + m:ST_RS + m + 1],
                                           stats[:, ST_RSP:ST_RSP + 8], axis=AX.X),
               waits=[("rsp", m, n2) for n2 in range(8)], ms=("rs", m))
        op("vector",
           lambda e: e.tensor_scalar_mul(stats[:, ST_NB:ST_NB + 8], stats[:, ST_RS:ST_RS + 8],
                                         stats[:, ST_NEGFS:ST_NEGFS + 1]),
           waits=[("rs", m2) for m2 in range(8)] + [("negfs",)], ms=("nb",))

        x2tall = ("x2g", 7, 7)

        X2RS = 3072
        for nn in range(8):
            for kk in range(KD):
                b = kk % 2
                if kk >= 2:
                    w = [("fwcast", nn, kk - 2)]
                elif nn > 0:
                    w = [("fwcast", nn - 1, KD - 2 + kk)]
                else:
                    w = [("nb",), x2tall]
                dma("sync", Bu[:, FWST + b * 512:FWST + (b + 1) * 512],
                    fw_d[kk * 128:(kk + 1) * 128, nn * 512:(nn + 1) * 512],
                    waits=w, ms=("fwd", nn, kk))
                wc = [("fwd", nn, kk)]
                if kk >= 2:
                    wc.append(("fmm", nn, kk - 2))
                elif nn > 0:
                    wc.append(("fmm", nn - 1, KD - 2 + kk))
                op("scalar",
                   lambda e, b=b: e.copy(Bb[:, FWBF + b * 512:FWBF + (b + 1) * 512],
                                         Bu[:, FWST + b * 512:FWST + (b + 1) * 512]),
                   waits=wc, ms=("fwcast", nn, kk))
                for m in range(8):
                    wmm = [("fwcast", nn, kk), x2tall]
                    if kk == 0 and nn > 0:
                        wmm.append(("z2ev", nn - 1, m))
                    op("tensor",
                       lambda e, m=m, b=b, kk=kk: e.matmul(
                           bank(m),
                           A[:, X2T0 + kk * 1024 + m * 128:X2T0 + kk * 1024 + (m + 1) * 128],
                           Bb[:, FWBF + b * 512:FWBF + (b + 1) * 512],
                           start=(kk == 0), stop=(kk == KD - 1)),
                       waits=wmm if kk == 0 else (wmm if m == 0 else ()),
                       ms=("fmm", nn, kk) if m == 7 else None)
            for m in range(8):
                zb = m % 4
                xb = m % 4
                wx = [("nb",), x2tall]
                if nn > 0 or m >= 4:
                    pm = (nn, m - 4) if m >= 4 else (nn - 1, m + 4)
                    wx.append(("xadj",) + pm)
                dma("sync", Bf[:, X2RS + xb * 512:X2RS + (xb + 1) * 512],
                    x2_hbm[m][:, nn * 512:(nn + 1) * 512],
                    waits=wx + [("x2dma", m, nn)], ms=("x2sl", nn, m))
                op("vector",
                   lambda e, m=m, xb=xb: e.tensor_scalar_add(
                       Bf[:, X2RS + xb * 512:X2RS + (xb + 1) * 512],
                       Bf[:, X2RS + xb * 512:X2RS + (xb + 1) * 512],
                       stats[:, ST_NB + m:ST_NB + m + 1]),
                   waits=[("x2sl", nn, m), ("nb",)], ms=("xadj", nn, m))
                w = [("fmm", nn, KD - 1), ("xadj", nn, m), ("fs2",)]
                if nn > 0 or m >= 4:
                    pm = (nn, m - 4) if m >= 4 else (nn - 1, m + 4)
                    w.append(("z2dma",) + pm)
                op("vector",
                   lambda e, m=m, zb=zb, xb=xb: e.scalar_tensor_tensor(
                       Bf[:, Z2ST + zb * 512:Z2ST + (zb + 1) * 512], bank(m),
                       stats[:, ST_FS2:ST_FS2 + 1],
                       Bf[:, X2RS + xb * 512:X2RS + (xb + 1) * 512],
                       ALU.mult, ALU.add),
                   waits=w, ms=("z2ev", nn, m))
                wb = [("z2ev", nn, m)]
                if nn == 0:
                    wb.append(("mv", 7))
                op("vector",
                   lambda e, m=m, nn=nn, zb=zb: e.bn_stats(
                       stats[:, ST_BN + (m * 8 + nn) * 6:ST_BN + (m * 8 + nn + 1) * 6],
                       Bf[:, Z2ST + zb * 512:Z2ST + (zb + 1) * 512]),
                   waits=wb, ms=("bnst2", m, nn))
                dma("sync", z2_hbm[m][:, nn * 512:(nn + 1) * 512],
                    Bf[:, Z2ST + zb * 512:Z2ST + (zb + 1) * 512],
                    waits=[("bnst2", m, nn)], ms=("z2dma", nn, m))

        z2all_list = [("z2dma", 7, i) for i in range(8)]

        T7, O7 = 0, 2048
        for m in range(8):
            op("vector",
               lambda e, m=m: e.bn_aggr(stats[:, ST_MV + m * 2:ST_MV + (m + 1) * 2],
                                        stats[:, ST_BN + m * 48:ST_BN + (m + 1) * 48]),
               waits=[("bnst2", m, n2) for n2 in range(8)] + [("rstd", 7), ("nmr", 7)],
               ms=("mv2", m))
            op("scalar",
               lambda e, m=m: e.activation(stats[:, ST_RSTD + m:ST_RSTD + m + 1],
                                           stats[:, ST_MV + m * 2 + 1:ST_MV + m * 2 + 2],
                                           AF.Ln, bias=stats[:, ST_EPS:ST_EPS + 1], scale=1.0),
               waits=[("mv2", m), ("epsv",)], ms=("lnv2", m))
            op("scalar",
               lambda e, m=m: e.activation(stats[:, ST_RSTD + m:ST_RSTD + m + 1],
                                           stats[:, ST_RSTD + m:ST_RSTD + m + 1],
                                           AF.Exp, bias=0.0, scale=-0.5),
               waits=[("lnv2", m)], ms=("rstd2", m))
            op("vector",
               lambda e, m=m: e.tensor_mul(stats[:, ST_NMR + m:ST_NMR + m + 1],
                                           stats[:, ST_MV + m * 2:ST_MV + m * 2 + 1],
                                           stats[:, ST_RSTD + m:ST_RSTD + m + 1]),
               waits=[("rstd2", m)], ms=("nmr2", m))
            for nn in range(8):
                b = nn % 4
                w = list(z2all_list) + [("z2dma", 7, m)]
                if nn >= 4:
                    w.append(("outn", m, nn - 4))
                elif m > 0:
                    w.append(("outn", m - 1, nn + 4))
                dma("sync", Bf[:, T7 + b * 512:T7 + (b + 1) * 512],
                    z2_hbm[m][:, nn * 512:(nn + 1) * 512], waits=w, ms=("z2rd", m, nn))
                tb = nn % 2
                wn = [("z2rd", m, nn), ("nmr2", m)]
                if nn >= 2:
                    wn.append(("outdma", m, nn - 2))
                elif m > 0:
                    wn.append(("outdma", m - 1, nn + 6))
                op("vector",
                   lambda e, m=m, b=b, tb=tb: e.tensor_scalar(
                       Bf[:, O7 + tb * 512:O7 + (tb + 1) * 512],
                       Bf[:, T7 + b * 512:T7 + (b + 1) * 512],
                       stats[:, ST_RSTD + m:ST_RSTD + m + 1],
                       stats[:, ST_NMR + m:ST_NMR + m + 1],
                       ALU.mult, ALU.subtract),
                   waits=wn, ms=("outn", m, nn))
                dma("sync", out_d[m * 128:(m + 1) * 128, nn * 512:(nn + 1) * 512],
                    Bf[:, O7 + tb * 512:O7 + (tb + 1) * 512],
                    waits=[("outn", m, nn)], ms=("outdma", m, nn))

        prev = ("outdma", 7, 7)
        if "qkv" in taps:
            dma("sync", tap_outs["tap_q"], qspill, waits=[prev], ms=("tapq",))
            dma("sync", tap_outs["tap_k"], kspill, waits=[("tapq",)], ms=("tapk",))
            dma("sync", tap_outs["tap_v"], vspill, waits=[("tapk",)], ms=("tapv",))
        if "cat" in taps:
            dma("sync", tap_outs["tap_cat"], A[:, 0:32768], waits=[prev], ms=("tapcat",))
        if "z" in taps:
            dma("sync", tap_outs["tap_z"], z_hbm, waits=[prev], ms=("tapz",))
        if "x2" in taps:
            dma("sync", tap_outs["tap_x2"], x2_hbm, waits=[prev], ms=("tapx2",))
        if "z2" in taps:
            dma("sync", tap_outs["tap_z2"], z2_hbm, waits=[prev], ms=("tapz2",))

        for i in range(QK["sync"]):
            E.final_wait("sync", f"s{i}")
        for i in range(QK["gpsimd"]):
            E.final_wait("gpsimd", f"g{i}")

    E.begin_count()
    walk()

    sems = {}
    with ExitStack() as semstack:
        for name in (["t", "v", "s", "p"] + [f"s{i}" for i in range(12)] + [f"g{i}" for i in range(8)]):
            sems[name] = semstack.enter_context(nc.semaphore(f"sem_{name}"))
        with nc.Block() as block:
            @block.sync
            def _(eng):
                E.begin_emit("sync", eng, sems)
                walk()

            @block.gpsimd
            def _(eng):
                E.begin_emit("gpsimd", eng, sems)
                walk()

            @block.tensor
            def _(eng):
                E.begin_emit("tensor", eng, sems)
                walk()

            @block.vector
            def _(eng):
                E.begin_emit("vector", eng, sems)
                walk()

            @block.scalar
            def _(eng):
                E.begin_emit("scalar", eng, sems)
                walk()
    es.close()
    return nc



def _rot_perm():
    p = np.arange(D)
    for h in range(NH):
        base = h * HD
        p[base:base + 64] = np.concatenate([np.arange(0, 64, 2), np.arange(1, 64, 2)]) + base
    return p


def _tables(idx):
    inv_freq = (1.0 / (10000.0 ** (np.arange(0, 64, 2, dtype=np.float32) / np.float32(64.0)))).astype(np.float32)
    t = idx.astype(np.float32)
    sinus = np.einsum("i,j->ij", t, inv_freq).astype(np.float32)
    sn = np.sin(sinus).astype(np.float32).T
    cs = np.cos(sinus).astype(np.float32).T
    out = np.zeros((128, len(idx)), np.float32)
    out[0:32] = cs
    out[32:64] = cs
    out[64:96] = sn
    out[96:128] = sn
    return out


def _in_maps(x, wq, wk, wv, wo, fw, fs):
    ident = np.eye(128, dtype=np.float32).astype(BFNP)
    mval = np.float32(NEGV / 16.0)
    tri = np.where(np.arange(128)[:, None] >= np.arange(128)[None, :],
                   np.float32(0.0), mval).astype(BFNP)
    mb = [np.full((128, 128), mval, np.float32).astype(BFNP),
          np.zeros((128, 128), np.float32).astype(BFNP)]
    maps, myidx_all = [], []
    for c in range(8):
        b, par = c // 2, c % 2
        myidx = np.concatenate([np.arange((2 * jj + par) * 128, (2 * jj + par + 1) * 128)
                                for jj in range(8)])
        pidx = np.concatenate([np.arange((2 * jj + 1 - par) * 128, (2 * jj + 2 - par) * 128)
                               for jj in range(8)])
        allidx = np.concatenate([myidx, pidx])
        myidx_all.append(myidx)
        maps.append({
            "xT": np.ascontiguousarray(x[b].T[:, allidx]),
            "xres": np.ascontiguousarray(x[b][myidx]),
            "wq": wq, "wk": wk, "wv": wv, "wo": wo, "fw": fw, "fs": fs,
            "tabq": _tables(myidx).astype(BFNP), "tabk": _tables(allidx).astype(BFNP),
            "maska": tri, "maskb": mb[par], "ident": ident,
        })
    return maps, myidx_all


def kernel(L0_attn_hidden_states, L0_attn_Wq, L0_attn_Wk, L0_attn_Wv, L0_attn_Wo,
           L0_ffn_kernel, L0_ffn_scale):
    x = np.asarray(L0_attn_hidden_states)
    perm = _rot_perm()
    wq = np.ascontiguousarray(np.asarray(L0_attn_Wq)[:, perm])
    wk = np.ascontiguousarray(np.asarray(L0_attn_Wk)[:, perm])
    wv = np.ascontiguousarray(np.asarray(L0_attn_Wv))
    wo = np.ascontiguousarray(np.asarray(L0_attn_Wo))
    fw = np.ascontiguousarray(np.asarray(L0_ffn_kernel).view(np.uint8))
    fs = np.asarray(L0_ffn_scale, dtype=np.float32).reshape(1, 1)

    nc = build_nc()
    in_maps, myidx_all = _in_maps(x, wq, wk, wv, wo, fw, fs)
    res = run_bass_kernel_spmd(nc, in_maps, core_ids=list(range(8)))
    out = np.empty((4, S, D), np.float32)
    for c in range(8):
        out[c // 2, myidx_all[c], :] = res.results[c]["out"]
    return out


# version 07
# speedup vs baseline: 1.1593x; 1.1593x over previous
from contextlib import ExitStack

import numpy as np
import ml_dtypes

import concourse.bass as bass
import concourse.mybir as mybir
from concourse.bass_utils import run_bass_kernel_spmd

F32 = mybir.dt.float32
BF16 = mybir.dt.bfloat16
U8 = mybir.dt.uint8
AF = mybir.ActivationFunctionType
ALU = mybir.AluOpType
AX = mybir.AxisListType

D = 4096
S = 2048
NH = 16
HD = 256
KD = D // 128
MYT = 1024
EPS = 1e-5
NEGV = -1e9
SCL = 1.0 / 16.0

BFNP = np.dtype(ml_dtypes.bfloat16)


def _split512(n):
    out = []
    while n > 0:
        c = min(n, 512)
        out.append(c)
        n -= c
    return out


class Sched:

    def __init__(self):
        self.mode = "count"
        self.cnt = {}
        self.ms = {}
        self.eng = None
        self.engname = None
        self.sems = None
        self.waited = None

    def begin_count(self):
        self.mode = "count"
        self.cnt = {}
        self.ms = {}

    def begin_emit(self, engname, eng, sems):
        self.mode = "emit"
        self.engname = engname
        self.eng = eng
        self.sems = sems
        self.waited = {}

    def op(self, engname, fn, waits=(), inc=None, ms=None):
        if self.mode == "count":
            for k in waits:
                assert k in self.ms, f"wait on undefined milestone {k}"
            if inc is not None:
                s, n = inc
                self.cnt[s] = self.cnt.get(s, 0) + n
                if ms is not None:
                    for k in (ms if isinstance(ms, list) else [ms]):
                        assert k not in self.ms, f"dup milestone {k}"
                        self.ms[k] = (s, self.cnt[s])
            else:
                assert ms is None, ms
            return
        if engname != self.engname:
            return
        for k in waits:
            s, v = self.ms[k]
            if v > self.waited.get(s, 0):
                self.eng.wait_ge(self.sems[s], v)
                self.waited[s] = v
        instr = fn(self.eng)
        if inc is not None:
            s, n = inc
            instr.then_inc(self.sems[s], n)

    def final_wait(self, engname, semname):
        if self.mode == "count" or engname != self.engname:
            return
        v = self.cnt.get(semname, 0)
        if v > self.waited.get(semname, 0):
            self.eng.wait_ge(self.sems[semname], v)
            self.waited[semname] = v


def build_nc(taps=()):
    nc = bass.Bass("TRN2", target_bir_lowering=False)

    xT_d = nc.dram_tensor("xT", [D, S], F32, kind="ExternalInput").ap()
    xres_d = nc.dram_tensor("xres", [MYT, D], F32, kind="ExternalInput").ap()
    wq_d = nc.dram_tensor("wq", [D, D], F32, kind="ExternalInput").ap()
    wk_d = nc.dram_tensor("wk", [D, D], F32, kind="ExternalInput").ap()
    wv_d = nc.dram_tensor("wv", [D, D], F32, kind="ExternalInput").ap()
    wo_d = nc.dram_tensor("wo", [D, D], F32, kind="ExternalInput").ap()
    fw_d = nc.dram_tensor("fw", [D, D], U8, kind="ExternalInput").ap()
    fs_d = nc.dram_tensor("fs", [1, 1], F32, kind="ExternalInput").ap()
    tabq_d = nc.dram_tensor("tabq", [128, MYT], BF16, kind="ExternalInput").ap()
    tabk_d = nc.dram_tensor("tabk", [128, S], BF16, kind="ExternalInput").ap()
    maska_d = nc.dram_tensor("maska", [128, 128], BF16, kind="ExternalInput").ap()
    maskb_d = nc.dram_tensor("maskb", [128, 128], BF16, kind="ExternalInput").ap()
    ident_d = nc.dram_tensor("ident", [128, 128], BF16, kind="ExternalInput").ap()
    out_d = nc.dram_tensor("out", [MYT, D], F32, kind="ExternalOutput").ap()

    qspill = nc.dram_tensor("qspill", [NH, HD, MYT], BF16, kind="Internal").ap()
    kspill = nc.dram_tensor("kspill", [NH, HD, S], BF16, kind="Internal").ap()
    vspill = nc.dram_tensor("vspill", [NH, 128, 16 * HD], BF16, kind="Internal").ap()
    z_hbm = nc.dram_tensor("z_hbm", [8, 128, D], F32, kind="Internal").ap()
    x2_hbm = nc.dram_tensor("x2_hbm", [8, 128, D], F32, kind="Internal").ap()
    z2_hbm = nc.dram_tensor("z2_hbm", [8, 128, D], F32, kind="Internal").ap()

    tap_outs = {}
    if "qkv" in taps:
        tap_outs["tap_q"] = nc.dram_tensor("tap_q", [NH, HD, MYT], BF16, kind="ExternalOutput").ap()
        tap_outs["tap_k"] = nc.dram_tensor("tap_k", [NH, HD, S], BF16, kind="ExternalOutput").ap()
        tap_outs["tap_v"] = nc.dram_tensor("tap_v", [NH, 128, 16 * HD], BF16, kind="ExternalOutput").ap()
    if "cat" in taps:
        tap_outs["tap_cat"] = nc.dram_tensor("tap_cat", [128, 32768], BF16, kind="ExternalOutput").ap()
    if "z" in taps:
        tap_outs["tap_z"] = nc.dram_tensor("tap_z", [8, 128, D], F32, kind="ExternalOutput").ap()
    if "x2" in taps:
        tap_outs["tap_x2"] = nc.dram_tensor("tap_x2", [8, 128, D], F32, kind="ExternalOutput").ap()
    if "z2" in taps:
        tap_outs["tap_z2"] = nc.dram_tensor("tap_z2", [8, 128, D], F32, kind="ExternalOutput").ap()

    es = ExitStack()
    A = es.enter_context(nc.sbuf_tensor("arenaA", [128, 65536], BF16))
    Bu = es.enter_context(nc.sbuf_tensor("arenaB", [128, 65536], U8))
    Bf = Bu.bitcast(F32)
    Bb = Bu.bitcast(BF16)
    identb = es.enter_context(nc.sbuf_tensor("identb", [128, 128], BF16))
    maska = es.enter_context(nc.sbuf_tensor("maska_s", [128, 128], BF16))
    maskb = es.enter_context(nc.sbuf_tensor("maskb_s", [128, 128], BF16))
    tabqc = es.enter_context(nc.sbuf_tensor("tabqc", [64, MYT], BF16))
    tabqs = es.enter_context(nc.sbuf_tensor("tabqs", [64, MYT], BF16))
    tabkc = es.enter_context(nc.sbuf_tensor("tabkc", [64, S], BF16))
    tabks = es.enter_context(nc.sbuf_tensor("tabks", [64, S], BF16))
    stats = es.enter_context(nc.sbuf_tensor("stats", [128, 512], F32))
    ones1 = es.enter_context(nc.sbuf_tensor("ones1", [1, 128], F32))

    ST_BN = 0
    ST_MV = 384
    ST_RSTD = 400
    ST_NMR = 408
    ST_RS = 416
    ST_NB = 424
    ST_FS = 432
    ST_FSB = 433
    ST_FS2 = 434
    ST_NEGFS = 435
    ST_EPS = 436
    ST_RSP = 440

    PL = es.enter_context(nc.psum_tensor("PL", [128, 2048], F32))
    pb = [es.enter_context(nc.psum_tensor(f"pb{i}", [128, 512], F32)) for i in range(4)]
    trP = [p.bitcast(BF16) for p in pb[:2]]
    avP = [pb[2], pb[3]]

    def bank(i):
        if i < 4:
            return PL[:, i * 512:(i + 1) * 512]
        return pb[i - 4][:, :]

    CAT0 = 0
    QKVB = 32768
    SCR = QKVB + 2 * 10240
    AU0 = SCR
    AN0 = SCR + 2048
    AT0 = SCR + 4096
    X2T0 = 32768

    XST, WQST, WKST, WVST = 0, 1024, 2048, 2560
    WQBF, WKBF, WVBF = 7168, 8192, 8704
    QSTG, KSTG, VSTG = 9728, 13824, 22016
    RPQ, RPK, SCQ, SCK = 0, 1024, 3072, 5120
    SV = 4608
    WOST, ZST = 0, 1024
    WOBF = 6144
    ZSL, XSL, TSL, X2F = 0, 2048, 4096, 5120
    X2BF = 12544
    FWST = 0
    FWBF = 512
    Z2ST = 5120
    OST = 5120

    E = Sched()

    def walk():
        QK = {"sync": 12, "gpsimd": 8}
        qc = {"sync": 0, "gpsimd": 0}

        def dma(q, out, in_, waits=(), ms=None):
            j = qc[q]
            qc[q] = j + 1
            sem = f"{q[0]}{j % QK[q]}"
            w = list(waits)
            if j >= QK[q]:
                w.append(("_dma", q, j - QK[q]))
            mss = [("_dma", q, j)] + ([ms] if ms is not None else [])
            E.op(q, lambda e, o=out, i=in_: e.dma_start(out=o, in_=i),
                 w, inc=(sem, 16), ms=mss)

        def op(eng, fn, waits=(), ms=None):
            semname = {"tensor": "t", "vector": "v", "scalar": "s", "gpsimd": "p"}[eng]
            E.op(eng, fn, waits, inc=(semname, 1) if ms is not None else None, ms=ms)

        dma("sync", identb[:, :], ident_d, ms=("d_ident",))
        dma("sync", maska[:, :], maska_d, ms=("d_maska",))
        dma("sync", maskb[:, :], maskb_d, ms=("d_maskb",))
        dma("sync", tabqc[:, :], tabq_d[0:64, :], ms=("tabq",))
        dma("sync", tabqs[:, :], tabq_d[64:128, :], ms=("tabqs",))
        dma("sync", tabkc[:, :], tabk_d[0:64, :], ms=("tabk",))
        dma("sync", tabks[:, :], tabk_d[64:128, :], ms=("tabks",))
        dma("sync", stats[0:1, ST_FS:ST_FS + 1], fs_d, ms=("d_fs",))
        op("vector", lambda e: e.memset(ones1[:, :], 1.0), ms=("ones",))
        op("vector", lambda e: e.memset(stats[:, ST_EPS:ST_EPS + 1], float(EPS)), ms=("epsv",))
        op("tensor", lambda e: e.matmul(pb[0][:, 0:1], ones1[0:1, :],
                                        stats[0:1, ST_FS:ST_FS + 1], start=True, stop=True),
           waits=[("ones",), ("d_fs",)], ms=("fsmm",))
        op("vector", lambda e: e.tensor_copy(stats[:, ST_FSB:ST_FSB + 1], pb[0][:, 0:1]),
           waits=[("fsmm",)], ms=("fsb",))
        op("vector", lambda e: e.tensor_scalar_mul(stats[:, ST_FS2:ST_FS2 + 1],
                                                   stats[:, ST_FSB:ST_FSB + 1], 2.0),
           waits=[("fsb",)], ms=("fs2",))
        op("vector", lambda e: e.tensor_scalar_mul(stats[:, ST_NEGFS:ST_NEGFS + 1],
                                                   stats[:, ST_FSB:ST_FSB + 1], -1.0),
           waits=[("fsb",)], ms=("negfs",))

        for ii in range(128):
            kk, qq = ii // 4, ii % 4
            b = ii % 2
            w = [("xcast", ii - 2)] if ii >= 2 else [("tabq",)]
            dma("gpsimd", Bf[:, XST + b * 512:XST + (b + 1) * 512],
                xT_d[kk * 128:(kk + 1) * 128, qq * 512:(qq + 1) * 512],
                waits=w, ms=("xdma", ii))
            op("scalar",
               lambda e, kk=kk, qq=qq, b=b: e.copy(
                   A[:, kk * 2048 + qq * 512:kk * 2048 + (qq + 1) * 512],
                   Bf[:, XST + b * 512:XST + (b + 1) * 512]),
               waits=[("xdma", ii)], ms=("xcast", ii))

        for p in range(8):
            h0 = 2 * p
            for kk in range(KD):
                b = kk % 2
                if kk >= 2:
                    w = [("qcast", p, kk - 2)]
                elif p > 0:
                    w = [("qmm", p - 1, KD - 1)]
                else:
                    w = [("tabk",)]
                dma("sync", Bf[:, WQST + b * 512:WQST + (b + 1) * 512],
                    wq_d[kk * 128:(kk + 1) * 128, h0 * HD:h0 * HD + 512],
                    waits=w, ms=("qwd", p, kk))
                wc = [("qwd", p, kk)]
                if kk >= 2:
                    wc.append(("qmm", p, kk - 2))
                op("vector",
                   lambda e, b=b: e.tensor_copy(Bb[:, WQBF + b * 512:WQBF + (b + 1) * 512],
                                                Bf[:, WQST + b * 512:WQST + (b + 1) * 512]),
                   waits=wc, ms=("qcast", p, kk))
                for m in range(4):
                    for n in range(2):
                        wmm = [("qcast", p, kk), ("xcast", min(kk * 4 + 3, 127))]
                        if kk == 0:
                            wmm.append(("vevb", p - 1, 1, m * 2 + n) if p > 0 else ("fsb",))
                        op("tensor",
                           lambda e, b=b, m=m, n=n, kk=kk: e.matmul(
                               bank(m * 2 + n),
                               Bb[:, WQBF + b * 512 + m * 128:WQBF + b * 512 + (m + 1) * 128],
                               A[:, kk * 2048 + n * 512:kk * 2048 + (n + 1) * 512],
                               start=(kk == 0), stop=(kk == KD - 1)),
                           waits=wmm if kk == 0 else (wmm if (m == 0 and n == 0) else ()),
                           ms=("qmm", p, kk) if (m == 3 and n == 1) else None)
            for m in range(4):
                for n in range(2):
                    w = [("qmm", p, KD - 1)]
                    if p > 0:
                        w += [("qspill", p - 1, 0), ("qspill", p - 1, 1)]
                    op("vector",
                       lambda e, m=m, n=n: e.tensor_scalar_mul(
                           Bb[:, QSTG + m * 1024 + n * 512:QSTG + m * 1024 + (n + 1) * 512],
                           bank(m * 2 + n), SCL),
                       waits=w, ms=("qevb", p, m * 2 + n))
            for hh in range(2):
                dma("gpsimd",
                    qspill[h0 + hh].rearrange("(c p) t -> p c t", p=128),
                    Bb[:, QSTG + hh * 2048:QSTG + (hh + 1) * 2048].rearrange("p (c t) -> p c t", t=1024),
                    waits=[("qevb", p, i) for i in range(8)], ms=("qspill", p, hh))

            for hh in range(2):
                h = h0 + hh
                for kk in range(KD):
                    b = kk % 2
                    if kk >= 2:
                        w = [("kcast", p, hh, kk - 2)]
                    elif hh == 1:
                        w = [("kmm", p, 0, KD - 2 + kk)]
                    elif p > 0:
                        w = [("kmm", p - 1, 1, KD - 2 + kk)]
                    else:
                        w = [("tabk",)]
                    dma("sync", Bf[:, WKST + b * 256:WKST + (b + 1) * 256],
                        wk_d[kk * 128:(kk + 1) * 128, h * HD:(h + 1) * HD],
                        waits=w, ms=("kwd", p, hh, kk))
                    wc = [("kwd", p, hh, kk)]
                    if kk >= 2:
                        wc.append(("kmm", p, hh, kk - 2))
                    op("vector",
                       lambda e, b=b: e.tensor_copy(Bb[:, WKBF + b * 256:WKBF + (b + 1) * 256],
                                                    Bf[:, WKST + b * 256:WKST + (b + 1) * 256]),
                       waits=wc, ms=("kcast", p, hh, kk))
                    for m in range(2):
                        for n in range(4):
                            wmm = [("kcast", p, hh, kk)]
                            if kk == 0:
                                wmm.append(("qevb", p, m * 4 + n) if hh == 0
                                           else ("kevb", p, 0, m * 4 + n))
                            op("tensor",
                               lambda e, b=b, m=m, n=n, kk=kk: e.matmul(
                                   bank(m * 4 + n),
                                   Bb[:, WKBF + b * 256 + m * 128:WKBF + b * 256 + (m + 1) * 128],
                                   A[:, kk * 2048 + n * 512:kk * 2048 + (n + 1) * 512],
                                   start=(kk == 0), stop=(kk == KD - 1)),
                               waits=wmm if kk == 0 else (wmm if (m == 0 and n == 0) else ()),
                               ms=("kmm", p, hh, kk) if (m == 1 and n == 3) else None)
                for m in range(2):
                    for n in range(4):
                        w = [("kmm", p, hh, KD - 1)]
                        if p > 0:
                            w.append(("kspill", p - 1, hh))
                        op("vector",
                           lambda e, hh=hh, m=m, n=n: e.tensor_copy(
                               Bb[:, KSTG + hh * 4096 + m * 2048 + n * 512:
                                  KSTG + hh * 4096 + m * 2048 + (n + 1) * 512],
                               bank(m * 4 + n)),
                           waits=w, ms=("kevb", p, hh, m * 4 + n))
                dma("gpsimd",
                    kspill[h].rearrange("(c p) t -> p c t", p=128),
                    Bb[:, KSTG + hh * 4096:KSTG + (hh + 1) * 4096].rearrange("p (c t) -> p c t", t=2048),
                    waits=[("kevb", p, hh, i) for i in range(8)], ms=("kspill", p, hh))

            for ps in range(2):
                for kk in range(KD):
                    b = kk % 2
                    if kk >= 2:
                        w = [("vcast", p, ps, kk - 2)]
                    elif ps == 1:
                        w = [("vmm", p, 0, KD - 2 + kk)]
                    elif p > 0:
                        w = [("vmm", p - 1, 1, KD - 2 + kk)]
                    else:
                        w = [("tabk",)]
                    dma("sync", Bf[:, WVST + b * 512:WVST + (b + 1) * 512],
                        wv_d[kk * 128:(kk + 1) * 128, h0 * HD:h0 * HD + 512],
                        waits=w, ms=("vwd", p, ps, kk))
                    wc = [("vwd", p, ps, kk)]
                    if kk >= 2:
                        wc.append(("vmm", p, ps, kk - 2))
                    op("vector",
                       lambda e, b=b: e.tensor_copy(Bb[:, WVBF + b * 512:WVBF + (b + 1) * 512],
                                                    Bf[:, WVST + b * 512:WVST + (b + 1) * 512]),
                       waits=wc, ms=("vcast", p, ps, kk))
                    for bidx in range(8):
                        c = ps * 8 + bidx
                        wmm = [("vcast", p, ps, kk)]
                        if kk == 0:
                            wmm.append(("kevb", p, 1, bidx) if ps == 0 else ("vevb", p, 0, bidx))
                        op("tensor",
                           lambda e, b=b, c=c, bidx=bidx, kk=kk: e.matmul(
                               bank(bidx),
                               A[:, kk * 2048 + c * 128:kk * 2048 + (c + 1) * 128],
                               Bb[:, WVBF + b * 512:WVBF + (b + 1) * 512],
                               start=(kk == 0), stop=(kk == KD - 1)),
                           waits=wmm if kk == 0 else (wmm if bidx == 0 else ()),
                           ms=("vmm", p, ps, kk) if bidx == 7 else None)
                for bidx in range(8):
                    c = ps * 8 + bidx
                    w = [("vmm", p, ps, KD - 1)]
                    if p > 0 and ps == 0:
                        w += [("vspill", p - 1, 0), ("vspill", p - 1, 1)]
                    op("vector",
                       lambda e, c=c, bidx=bidx: e.tensor_copy(
                           Bb[:, VSTG + c * 512:VSTG + (c + 1) * 512], bank(bidx)),
                       waits=w, ms=("vevb", p, ps, bidx))
            for hh in range(2):
                h = h0 + hh
                dma("gpsimd",
                    vspill[h].rearrange("p (c d) -> p c d", d=HD),
                    Bb[:, VSTG:VSTG + 16 * 512].rearrange("p (c two) -> p c two", two=512)[:, :, hh * HD:(hh + 1) * HD],
                    waits=[("vevb", p, 0, i) for i in range(8)] + [("vevb", p, 1, i) for i in range(8)],
                    ms=("vspill", p, hh))

        p1done = ("vspill", 7, 1)
        p1all = [("vspill", 7, 0), ("vspill", 7, 1)]

        grp_hist = []

        def qkvset(h):
            si = h % 3
            if si < 2:
                return A, QKVB + si * 10240
            return Bb, 9344

        def emit_head_loads(h):
            AR, qb = qkvset(h)
            kb = qb + 2048
            vb = qb + 6144
            wl = [("qspill", h // 2, h % 2), ("kspill", h // 2, h % 2),
                  ("vspill", h // 2, h % 2)] + p1all
            if h >= 3:
                wl.append(("avmm", (h - 3) * 8 + 7 + 1 - 1))
            dma("gpsimd", AR[:, qb:qb + 2048].rearrange("p (c t) -> p c t", t=1024),
                qspill[h].rearrange("(c p) t -> p c t", p=128),
                waits=wl, ms=("ldq", h))
            dma("gpsimd", AR[:, kb:kb + 4096].rearrange("p (c t) -> p c t", t=2048),
                kspill[h].rearrange("(c p) t -> p c t", p=128),
                waits=wl, ms=("ldk", h))
            dma("gpsimd", AR[:, vb:vb + 4096], vspill[h],
                waits=wl, ms=("ldv", h))
            wr = [("ldq", h)] + ([("ropeq", h - 1)] if h >= 1 else [])
            dma("gpsimd", Bb[0:32, 3072:4096], AR[32:64, qb:qb + 1024], waits=wr, ms=("swq1", h))
            dma("gpsimd", Bb[32:64, 3072:4096], AR[0:32, qb:qb + 1024], waits=wr, ms=("swq2", h))
            wr = [("ldk", h)] + ([("ropek", h - 1)] if h >= 1 else [])
            dma("gpsimd", Bb[0:32, 5120:7168], AR[32:64, kb:kb + 2048], waits=wr, ms=("swk1", h))
            dma("gpsimd", Bb[32:64, 5120:7168], AR[0:32, kb:kb + 2048], waits=wr, ms=("swk2", h))
            wv_ = [("swq1", h), ("swq2", h), ("tabq",), ("tabqs",)] + ([("ropeq", h - 1)] if h >= 1 else [])
            op("gpsimd", lambda e, AR=AR, qb=qb: e.tensor_mul(Bb[0:64, 4096:5120],
                                                              AR[0:64, qb:qb + 1024], tabqc[:, :]),
               waits=wv_, ms=("rq1", h))
            op("gpsimd", lambda e: e.tensor_mul(Bb[0:64, 3072:4096],
                                                Bb[0:64, 3072:4096], tabqs[:, :]),
               waits=[("rq1", h)] + wv_, ms=("rq2", h))
            op("gpsimd", lambda e, AR=AR, qb=qb: e.tensor_add(AR[0:64, qb:qb + 1024],
                                                              Bb[0:64, 4096:5120],
                                                              Bb[0:64, 3072:4096]),
               waits=[("rq1", h), ("rq2", h)], ms=("ropeq", h))
            wv_ = [("swk1", h), ("swk2", h), ("tabk",), ("tabks",)] + ([("ropek", h - 1)] if h >= 1 else [])
            op("gpsimd", lambda e, AR=AR, kb=kb: e.tensor_mul(Bb[0:64, 7168:9216],
                                                              AR[0:64, kb:kb + 2048], tabkc[:, :]),
               waits=wv_, ms=("rk1", h))
            op("gpsimd", lambda e: e.tensor_mul(Bb[0:64, 5120:7168],
                                                Bb[0:64, 5120:7168], tabks[:, :]),
               waits=[("rk1", h)] + wv_, ms=("rk2", h))
            op("gpsimd", lambda e, AR=AR, kb=kb: e.tensor_add(AR[0:64, kb:kb + 2048],
                                                              Bb[0:64, 7168:9216],
                                                              Bb[0:64, 5120:7168]),
               waits=[("rk1", h), ("rk2", h)], ms=("ropek", h))

        def emit_lg(t, P):
            h, jj = divmod(t, 8)
            AR, qb = qkvset(h)
            kb = qb + 2048
            klen = jj + 1
            KW = klen * 128
            wlog = [("ropeq", h), ("ropek", h), ("d_maska",), ("d_maskb",), ("d_ident",)]
            wlog.append(("expp", t - 1, P) if t > 0 else p1done)
            first = True
            pbase = P * 1024
            koff = P * 1024
            coff = 0
            chunks = _split512(KW)
            for ci, csz in enumerate(chunks):
                isdiag = (ci == len(chunks) - 1)
                for m in range(2):
                    stop = (m == 1 and not isdiag)
                    op("tensor",
                       lambda e, m=m, coff=coff, csz=csz, pbase=pbase, koff=koff,
                       qb=qb, kb=kb, jj=jj, stop=stop, AR=AR: e.matmul(
                           PL[:, pbase + coff:pbase + coff + csz],
                           AR[:, qb + m * 1024 + jj * 128:qb + m * 1024 + (jj + 1) * 128],
                           AR[:, kb + m * 2048 + koff + coff:kb + m * 2048 + koff + coff + csz],
                           start=(m == 0), stop=stop),
                       waits=wlog if first else ())
                    first = False
                if isdiag:
                    mk = maska if P == 0 else maskb
                    mend = pbase + KW
                    op("tensor",
                       lambda e, mk=mk, mend=mend: e.matmul(
                           PL[:, mend - 128:mend], identb[:, :], mk[:, :],
                           start=False, stop=True),
                       ms=("lgp", t, P))
                coff += csz

        def emit_mx(t):
            h, jj = divmod(t, 8)
            gp = t % 3
            KW = (jj + 1) * 128
            SVX = 4608 + 16 * gp
            wmx = [("lgp", t, 0)]
            if t >= 3:
                wmx.append(("smin", t - 3))
            op("vector", lambda e, KW=KW, SVX=SVX: e.reduce_max(
                   Bf[:, SVX:SVX + 1], PL[:, 0:KW], axis=AX.X, negate=True),
               waits=wmx, ms=("mxa", t))
            op("vector", lambda e, KW=KW, SVX=SVX: e.reduce_max(
                   Bf[:, SVX + 1:SVX + 2], PL[:, 1024:1024 + KW], axis=AX.X, negate=True),
               waits=[("lgp", t, 1), ("mxa", t)], ms=("mxb", t))

        def emit_smx(t):
            h, jj = divmod(t, 8)
            gp = t % 3
            KW = (jj + 1) * 128
            W = 2 * KW
            SVX = 4608 + 16 * gp
            AUB = 21632 + 2048 * gp
            AN = SCR + 2048 * gp
            wmin = [("mxa", t), ("mxb", t)]
            if t >= 3:
                wmin.append(("expp", t - 3, 1))
            op("vector", lambda e, SVX=SVX: e.tensor_tensor(
                   Bf[:, SVX + 2:SVX + 3], Bf[:, SVX:SVX + 1],
                   Bf[:, SVX + 1:SVX + 2], ALU.min),
               waits=wmin, ms=("smin", t))
            for P in range(2):
                wexp = [("smin", t)]
                if t >= 3:
                    wexp += [("nrm", t - 3), ("sadd", t - 3)]
                op("scalar", lambda e, P=P, KW=KW, SVX=SVX, AUB=AUB: e.activation(
                       Bb[:, AUB + P * KW:AUB + (P + 1) * KW],
                       PL[:, P * 1024:P * 1024 + KW], AF.Exp,
                       bias=Bf[:, SVX + 2:SVX + 3], scale=1.0,
                       accum_out=Bf[:, SVX + 3 + P:SVX + 4 + P]),
                   waits=wexp, ms=("expp", t, P))
            op("vector", lambda e, SVX=SVX: e.tensor_add(
                   Bf[:, SVX + 5:SVX + 6], Bf[:, SVX + 3:SVX + 4],
                   Bf[:, SVX + 4:SVX + 5]),
               waits=[("expp", t, 0), ("expp", t, 1)], ms=("sadd", t))
            op("vector", lambda e, SVX=SVX: e.reciprocal(
                   Bf[:, SVX + 6:SVX + 7], Bf[:, SVX + 5:SVX + 6]),
               waits=[("sadd", t)], ms=("rcp", t))
            wn = [("rcp", t)]
            if t >= 3:
                wn.append(("trl", *divmod(t - 3, 8)))
            op("vector", lambda e, W=W, AUB=AUB, AN=AN, SVX=SVX: e.tensor_scalar_mul(
                   A[:, AN:AN + W], Bb[:, AUB:AUB + W], Bf[:, SVX + 6:SVX + 7]),
               waits=wn, ms=("nrm", t))

        def emit_trav(t):
            h, jj = divmod(t, 8)
            AR, qb0 = qkvset(h)
            vb = qb0 + 6144
            gp = t % 3
            klen = jj + 1
            KK = 2 * klen
            AN = SCR + 2048 * gp
            AT = SCR + 6144 + 2048 * gp
            ngrp = (KK + 3) // 4
            for g in range(ngrp):
                kks = list(range(4 * g, min(4 * g + 4, KK)))
                tb = len(grp_hist) % 2
                wtr = [("nrm", t)]
                if len(grp_hist) >= 2:
                    wtr.append(grp_hist[-2])
                for i, kk in enumerate(kks):
                    last_of_task = (kk == KK - 1)
                    op("tensor",
                       lambda e, tb=tb, kk=kk, i=i, AN=AN: e.transpose(
                           trP[tb][:, i * 128:(i + 1) * 128],
                           A[:, AN + kk * 128:AN + (kk + 1) * 128], identb[:, :]),
                       waits=wtr if i == 0 else (),
                       ms=("trl", h, jj) if last_of_task else ("trx", h, jj, kk))
                wge = [("trl", h, jj) if kks[-1] == KK - 1 else ("trx", h, jj, kks[-1])]
                if t >= 3:
                    wge.append(("avmm", t - 3))
                gkey = ("grpev", t, g)
                op("vector",
                   lambda e, tb=tb, g=g, n=len(kks), AT=AT: e.tensor_copy(
                       A[:, AT + g * 512:AT + g * 512 + n * 128],
                       trP[tb][:, 0:n * 128]),
                   waits=wge, ms=gkey)
                grp_hist.append(gkey)
            for kk in range(KK):
                c = kk if kk < klen else 8 + (kk - klen)
                wav = [("grpev", t, kk // 4)]
                if kk == 0:
                    wav.append(("avev", t - 1) if t > 0 else p1done)
                    wav.append(("ldv", h))
                for dh in range(2):
                    op("tensor",
                       lambda e, dh=dh, c=c, kk=kk, KK=KK, vb=vb, AT=AT, AR=AR: e.matmul(
                           avP[dh][:, 0:128],
                           AR[:, vb + c * 256 + dh * 128:vb + c * 256 + (dh + 1) * 128],
                           A[:, AT + kk * 128:AT + (kk + 1) * 128],
                           start=(kk == 0), stop=(kk == KK - 1)),
                       waits=wav if (dh == 0 or kk == 0) else (),
                       ms=("avmm", t) if (kk == KK - 1 and dh == 1) else None)
            for dh in range(2):
                op("scalar",
                   lambda e, dh=dh, h=h, jj=jj: e.copy(
                       A[:, CAT0 + (2 * h + dh) * 1024 + jj * 128:
                          CAT0 + (2 * h + dh) * 1024 + (jj + 1) * 128],
                       avP[dh][:, 0:128]),
                   waits=[("avmm", t)], ms=("avev", t) if dh == 1 else None)

        emit_head_loads(0)
        emit_head_loads(1)
        emit_head_loads(2)
        T2 = NH * 8
        for u in range(T2 + 3):
            if u < T2:
                h, jj = divmod(u, 8)
                if jj == 3 and 3 <= h + 2 < NH:
                    emit_head_loads(h + 2)
                emit_lg(u, 0)
                emit_lg(u, 1)
                emit_mx(u)
            if u - 3 >= 0:
                emit_trav(u - 3)
            if u < T2:
                emit_smx(u)

        p2done = ("avev", NH * 8 - 1)

        XRS = 3584
        for nn in range(8):
            for kk in range(KD):
                b = kk % 2
                if kk >= 2:
                    w = [("wocast", nn, kk - 2)]
                elif nn > 0:
                    w = [("wocast", nn - 1, KD - 2 + kk)]
                else:
                    w = [p2done]
                dma("sync", Bf[:, WOST + b * 512:WOST + (b + 1) * 512],
                    wo_d[kk * 128:(kk + 1) * 128, nn * 512:(nn + 1) * 512],
                    waits=w, ms=("wod", nn, kk))
                wc = [("wod", nn, kk)]
                if kk >= 2:
                    wc.append(("womm", nn, kk - 2))
                elif nn > 0:
                    wc.append(("womm", nn - 1, KD - 2 + kk))
                op("scalar",
                   lambda e, b=b: e.copy(Bb[:, WOBF + b * 512:WOBF + (b + 1) * 512],
                                         Bf[:, WOST + b * 512:WOST + (b + 1) * 512]),
                   waits=wc, ms=("wocast", nn, kk))
                for m in range(8):
                    wmm = [("wocast", nn, kk), p2done]
                    if kk == 0 and nn > 0:
                        wmm.append(("zev", nn - 1, m))
                    op("tensor",
                       lambda e, m=m, b=b, kk=kk: e.matmul(
                           bank(m),
                           A[:, CAT0 + kk * 1024 + m * 128:CAT0 + kk * 1024 + (m + 1) * 128],
                           Bb[:, WOBF + b * 512:WOBF + (b + 1) * 512],
                           start=(kk == 0), stop=(kk == KD - 1)),
                       waits=wmm if kk == 0 else (wmm if m == 0 else ()),
                       ms=("womm", nn, kk) if m == 7 else None)
            for m in range(8):
                zb = m % 4
                xb = m % 4
                wx = [p2done]
                if nn > 0 or m >= 4:
                    pm = (nn, m - 4) if m >= 4 else (nn - 1, m + 4)
                    wx.append(("zev",) + pm)
                dma("sync", Bf[:, XRS + xb * 512:XRS + (xb + 1) * 512],
                    xres_d[m * 128:(m + 1) * 128, nn * 512:(nn + 1) * 512],
                    waits=wx, ms=("xrsd", nn, m))
                w = [("womm", nn, KD - 1), ("xrsd", nn, m)]
                if nn > 0 or m >= 4:
                    pm = (nn, m - 4) if m >= 4 else (nn - 1, m + 4)
                    w.append(("zdma",) + pm)
                op("vector",
                   lambda e, m=m, zb=zb, xb=xb: e.tensor_add(
                       Bf[:, ZST + zb * 512:ZST + (zb + 1) * 512], bank(m),
                       Bf[:, XRS + xb * 512:XRS + (xb + 1) * 512]),
                   waits=w, ms=("zev", nn, m))
                op("vector",
                   lambda e, m=m, nn=nn, zb=zb: e.bn_stats(
                       stats[:, ST_BN + (m * 8 + nn) * 6:ST_BN + (m * 8 + nn + 1) * 6],
                       Bf[:, ZST + zb * 512:ZST + (zb + 1) * 512]),
                   waits=[("zev", nn, m)], ms=("bnst", m, nn))
                dma("gpsimd", z_hbm[m][:, nn * 512:(nn + 1) * 512],
                    Bf[:, ZST + zb * 512:ZST + (zb + 1) * 512],
                    waits=[("bnst", m, nn)], ms=("zdma", nn, m))

        zall_list = [("zdma", 7, i) for i in range(8)]

        TSL5, X2F5, X2B5 = 0, 2048, 6144
        for m in range(8):
            op("vector",
               lambda e, m=m: e.bn_aggr(stats[:, ST_MV + m * 2:ST_MV + (m + 1) * 2],
                                        stats[:, ST_BN + m * 48:ST_BN + (m + 1) * 48]),
               waits=[("bnst", m, n2) for n2 in range(8)], ms=("mv", m))
            op("scalar",
               lambda e, m=m: e.activation(stats[:, ST_RSTD + m:ST_RSTD + m + 1],
                                           stats[:, ST_MV + m * 2 + 1:ST_MV + m * 2 + 2],
                                           AF.Ln, bias=stats[:, ST_EPS:ST_EPS + 1], scale=1.0),
               waits=[("mv", m), ("epsv",)], ms=("lnv", m))
            op("scalar",
               lambda e, m=m: e.activation(stats[:, ST_RSTD + m:ST_RSTD + m + 1],
                                           stats[:, ST_RSTD + m:ST_RSTD + m + 1],
                                           AF.Exp, bias=0.0, scale=-0.5),
               waits=[("lnv", m)], ms=("rstd", m))
            op("vector",
               lambda e, m=m: e.tensor_mul(stats[:, ST_NMR + m:ST_NMR + m + 1],
                                           stats[:, ST_MV + m * 2:ST_MV + m * 2 + 1],
                                           stats[:, ST_RSTD + m:ST_RSTD + m + 1]),
               waits=[("rstd", m)], ms=("nmr", m))
            for nn in range(8):
                b = nn % 4
                w = list(zall_list) + [("zdma", 7, m)]
                if nn >= 4:
                    w.append(("x2n", m, nn - 4))
                elif m > 0:
                    w.append(("x2n", m - 1, nn + 4))
                dma("sync", Bf[:, TSL5 + b * 512:TSL5 + (b + 1) * 512],
                    z_hbm[m][:, nn * 512:(nn + 1) * 512], waits=w, ms=("zrd", m, nn))
                tb = nn % 2
                wn = [("zrd", m, nn), ("nmr", m)]
                if nn >= 2:
                    wn.append(("x2dma", m, nn - 2))
                elif m > 0:
                    wn.append(("x2dma", m - 1, nn + 6))
                op("vector",
                   lambda e, m=m, b=b, tb=tb: e.tensor_scalar(
                       Bf[:, X2F5 + tb * 512:X2F5 + (tb + 1) * 512],
                       Bf[:, TSL5 + b * 512:TSL5 + (b + 1) * 512],
                       stats[:, ST_RSTD + m:ST_RSTD + m + 1],
                       stats[:, ST_NMR + m:ST_NMR + m + 1],
                       ALU.mult, ALU.subtract),
                   waits=wn, ms=("x2n", m, nn))
                dma("gpsimd", x2_hbm[m][:, nn * 512:(nn + 1) * 512],
                    Bf[:, X2F5 + tb * 512:X2F5 + (tb + 1) * 512],
                    waits=[("x2n", m, nn)], ms=("x2dma", m, nn))
                wxc = [("x2n", m, nn)]
                if nn >= 2:
                    wxc.append(("x2t", m, (nn - 2) * 4 + 3))
                elif m > 0:
                    wxc.append(("x2t", m - 1, (nn + 6) * 4 + 3))
                op("scalar",
                   lambda e, tb=tb: e.copy(
                       Bb[:, X2B5 + tb * 512:X2B5 + (tb + 1) * 512],
                       Bf[:, X2F5 + tb * 512:X2F5 + (tb + 1) * 512]),
                   waits=wxc, ms=("x2c", m, nn))
                wrsp = [("x2c", m, nn)]
                if m > 0 and nn == 0:
                    wrsp.append(("rs", m - 1))
                op("vector",
                   lambda e, nn=nn, tb=tb: e.reduce_sum(
                       stats[:, ST_RSP + nn:ST_RSP + nn + 1],
                       Bb[:, X2B5 + tb * 512:X2B5 + (tb + 1) * 512], axis=AX.X),
                   waits=wrsp, ms=("rsp", m, nn))
                pbk = len(grp_hist) % 2
                wtr = [("x2c", m, nn)]
                if len(grp_hist) >= 2:
                    wtr.append(grp_hist[-2])
                for t4 in range(4):
                    kkc = nn * 4 + t4
                    op("tensor",
                       lambda e, pbk=pbk, tb=tb, t4=t4: e.transpose(
                           trP[pbk][:, t4 * 128:(t4 + 1) * 128],
                           Bb[:, X2B5 + tb * 512 + t4 * 128:X2B5 + tb * 512 + (t4 + 1) * 128],
                           identb[:, :]),
                       waits=wtr if t4 == 0 else (), ms=("x2t", m, kkc))
                gkey = ("x2g", m, nn)
                op("vector",
                   lambda e, pbk=pbk, m=m, nn=nn: e.tensor_copy(
                       A.rearrange("p (c t) -> p c t", t=1024)[:, 32 + nn * 4:36 + nn * 4,
                                                              m * 128:(m + 1) * 128],
                       trP[pbk][:, 0:512].rearrange("p (c t) -> p c t", t=128)),
                   waits=[("x2t", m, nn * 4 + 3)], ms=gkey)
                grp_hist.append(gkey)
            op("vector",
               lambda e, m=m: e.reduce_sum(stats[:, ST_RS + m:ST_RS + m + 1],
                                           stats[:, ST_RSP:ST_RSP + 8], axis=AX.X),
               waits=[("rsp", m, n2) for n2 in range(8)], ms=("rs", m))
        op("vector",
           lambda e: e.tensor_scalar_mul(stats[:, ST_NB:ST_NB + 8], stats[:, ST_RS:ST_RS + 8],
                                         stats[:, ST_NEGFS:ST_NEGFS + 1]),
           waits=[("rs", m2) for m2 in range(8)] + [("negfs",)], ms=("nb",))

        x2tall = ("x2g", 7, 7)

        X2RS = 3072
        for nn in range(8):
            for kk in range(KD):
                b = kk % 2
                if kk >= 2:
                    w = [("fwcast", nn, kk - 2)]
                elif nn > 0:
                    w = [("fwcast", nn - 1, KD - 2 + kk)]
                else:
                    w = [("nb",), x2tall]
                dma("sync", Bu[:, FWST + b * 512:FWST + (b + 1) * 512],
                    fw_d[kk * 128:(kk + 1) * 128, nn * 512:(nn + 1) * 512],
                    waits=w, ms=("fwd", nn, kk))
                wc = [("fwd", nn, kk)]
                if kk >= 2:
                    wc.append(("fmm", nn, kk - 2))
                elif nn > 0:
                    wc.append(("fmm", nn - 1, KD - 2 + kk))
                op("scalar",
                   lambda e, b=b: e.copy(Bb[:, FWBF + b * 512:FWBF + (b + 1) * 512],
                                         Bu[:, FWST + b * 512:FWST + (b + 1) * 512]),
                   waits=wc, ms=("fwcast", nn, kk))
                for m in range(8):
                    wmm = [("fwcast", nn, kk), x2tall]
                    if kk == 0 and nn > 0:
                        wmm.append(("z2ev", nn - 1, m))
                    op("tensor",
                       lambda e, m=m, b=b, kk=kk: e.matmul(
                           bank(m),
                           A[:, X2T0 + kk * 1024 + m * 128:X2T0 + kk * 1024 + (m + 1) * 128],
                           Bb[:, FWBF + b * 512:FWBF + (b + 1) * 512],
                           start=(kk == 0), stop=(kk == KD - 1)),
                       waits=wmm if kk == 0 else (wmm if m == 0 else ()),
                       ms=("fmm", nn, kk) if m == 7 else None)
            for m in range(8):
                zb = m % 4
                xb = m % 4
                wx = [("nb",), x2tall]
                if nn > 0 or m >= 4:
                    pm = (nn, m - 4) if m >= 4 else (nn - 1, m + 4)
                    wx.append(("z2ev",) + pm)
                dma("sync", Bf[:, X2RS + xb * 512:X2RS + (xb + 1) * 512],
                    x2_hbm[m][:, nn * 512:(nn + 1) * 512],
                    waits=wx + [("x2dma", m, nn)], ms=("x2sl", nn, m))
                op("vector",
                   lambda e, m=m, xb=xb: e.tensor_scalar_add(
                       Bf[:, X2RS + xb * 512:X2RS + (xb + 1) * 512],
                       Bf[:, X2RS + xb * 512:X2RS + (xb + 1) * 512],
                       stats[:, ST_NB + m:ST_NB + m + 1]),
                   waits=[("x2sl", nn, m), ("nb",)], ms=("xadj", nn, m))
                w = [("fmm", nn, KD - 1), ("xadj", nn, m), ("x2sl", nn, m), ("fs2",)]
                if nn > 0 or m >= 4:
                    pm = (nn, m - 4) if m >= 4 else (nn - 1, m + 4)
                    w.append(("z2dma",) + pm)
                op("vector",
                   lambda e, m=m, zb=zb, xb=xb: e.scalar_tensor_tensor(
                       Bf[:, Z2ST + zb * 512:Z2ST + (zb + 1) * 512], bank(m),
                       stats[:, ST_FS2:ST_FS2 + 1],
                       Bf[:, X2RS + xb * 512:X2RS + (xb + 1) * 512],
                       ALU.mult, ALU.add),
                   waits=w, ms=("z2ev", nn, m))
                wb = [("z2ev", nn, m)]
                if nn == 0:
                    wb.append(("mv", 7))
                op("vector",
                   lambda e, m=m, nn=nn, zb=zb: e.bn_stats(
                       stats[:, ST_BN + (m * 8 + nn) * 6:ST_BN + (m * 8 + nn + 1) * 6],
                       Bf[:, Z2ST + zb * 512:Z2ST + (zb + 1) * 512]),
                   waits=wb, ms=("bnst2", m, nn))
                dma("gpsimd", z2_hbm[m][:, nn * 512:(nn + 1) * 512],
                    Bf[:, Z2ST + zb * 512:Z2ST + (zb + 1) * 512],
                    waits=[("bnst2", m, nn)], ms=("z2dma", nn, m))

        z2all_list = [("z2dma", 7, i) for i in range(8)]

        T7, O7 = 0, 2048
        for m in range(8):
            op("vector",
               lambda e, m=m: e.bn_aggr(stats[:, ST_MV + m * 2:ST_MV + (m + 1) * 2],
                                        stats[:, ST_BN + m * 48:ST_BN + (m + 1) * 48]),
               waits=[("bnst2", m, n2) for n2 in range(8)] + [("rstd", 7), ("nmr", 7)],
               ms=("mv2", m))
            op("scalar",
               lambda e, m=m: e.activation(stats[:, ST_RSTD + m:ST_RSTD + m + 1],
                                           stats[:, ST_MV + m * 2 + 1:ST_MV + m * 2 + 2],
                                           AF.Ln, bias=stats[:, ST_EPS:ST_EPS + 1], scale=1.0),
               waits=[("mv2", m), ("epsv",)], ms=("lnv2", m))
            op("scalar",
               lambda e, m=m: e.activation(stats[:, ST_RSTD + m:ST_RSTD + m + 1],
                                           stats[:, ST_RSTD + m:ST_RSTD + m + 1],
                                           AF.Exp, bias=0.0, scale=-0.5),
               waits=[("lnv2", m)], ms=("rstd2", m))
            op("vector",
               lambda e, m=m: e.tensor_mul(stats[:, ST_NMR + m:ST_NMR + m + 1],
                                           stats[:, ST_MV + m * 2:ST_MV + m * 2 + 1],
                                           stats[:, ST_RSTD + m:ST_RSTD + m + 1]),
               waits=[("rstd2", m)], ms=("nmr2", m))
            for nn in range(8):
                b = nn % 4
                w = list(z2all_list) + [("z2dma", 7, m)]
                if nn >= 4:
                    w.append(("outn", m, nn - 4))
                elif m > 0:
                    w.append(("outn", m - 1, nn + 4))
                dma("sync", Bf[:, T7 + b * 512:T7 + (b + 1) * 512],
                    z2_hbm[m][:, nn * 512:(nn + 1) * 512], waits=w, ms=("z2rd", m, nn))
                tb = nn % 2
                wn = [("z2rd", m, nn), ("nmr2", m)]
                if nn >= 2:
                    wn.append(("outdma", m, nn - 2))
                elif m > 0:
                    wn.append(("outdma", m - 1, nn + 6))
                op("vector",
                   lambda e, m=m, b=b, tb=tb: e.tensor_scalar(
                       Bf[:, O7 + tb * 512:O7 + (tb + 1) * 512],
                       Bf[:, T7 + b * 512:T7 + (b + 1) * 512],
                       stats[:, ST_RSTD + m:ST_RSTD + m + 1],
                       stats[:, ST_NMR + m:ST_NMR + m + 1],
                       ALU.mult, ALU.subtract),
                   waits=wn, ms=("outn", m, nn))
                dma("gpsimd", out_d[m * 128:(m + 1) * 128, nn * 512:(nn + 1) * 512],
                    Bf[:, O7 + tb * 512:O7 + (tb + 1) * 512],
                    waits=[("outn", m, nn)], ms=("outdma", m, nn))

        prev = ("outdma", 7, 7)
        if "qkv" in taps:
            dma("sync", tap_outs["tap_q"], qspill, waits=[prev], ms=("tapq",))
            dma("sync", tap_outs["tap_k"], kspill, waits=[("tapq",)], ms=("tapk",))
            dma("sync", tap_outs["tap_v"], vspill, waits=[("tapk",)], ms=("tapv",))
        if "cat" in taps:
            dma("sync", tap_outs["tap_cat"], A[:, 0:32768], waits=[prev], ms=("tapcat",))
        if "z" in taps:
            dma("sync", tap_outs["tap_z"], z_hbm, waits=[prev], ms=("tapz",))
        if "x2" in taps:
            dma("sync", tap_outs["tap_x2"], x2_hbm, waits=[prev], ms=("tapx2",))
        if "z2" in taps:
            dma("sync", tap_outs["tap_z2"], z2_hbm, waits=[prev], ms=("tapz2",))

        for i in range(QK["sync"]):
            E.final_wait("sync", f"s{i}")
        for i in range(QK["gpsimd"]):
            E.final_wait("gpsimd", f"g{i}")

    E.begin_count()
    walk()

    sems = {}
    with ExitStack() as semstack:
        for name in (["t", "v", "s", "p"] + [f"s{i}" for i in range(12)] + [f"g{i}" for i in range(8)]):
            sems[name] = semstack.enter_context(nc.semaphore(f"sem_{name}"))
        with nc.Block() as block:
            @block.sync
            def _(eng):
                E.begin_emit("sync", eng, sems)
                walk()

            @block.gpsimd
            def _(eng):
                E.begin_emit("gpsimd", eng, sems)
                walk()

            @block.tensor
            def _(eng):
                E.begin_emit("tensor", eng, sems)
                walk()

            @block.vector
            def _(eng):
                E.begin_emit("vector", eng, sems)
                walk()

            @block.scalar
            def _(eng):
                E.begin_emit("scalar", eng, sems)
                walk()
    es.close()
    return nc



def _rot_perm():
    p = np.arange(D)
    for h in range(NH):
        base = h * HD
        p[base:base + 64] = np.concatenate([np.arange(0, 64, 2), np.arange(1, 64, 2)]) + base
    return p


def _tables(idx):
    inv_freq = (1.0 / (10000.0 ** (np.arange(0, 64, 2, dtype=np.float32) / np.float32(64.0)))).astype(np.float32)
    t = idx.astype(np.float32)
    sinus = np.einsum("i,j->ij", t, inv_freq).astype(np.float32)
    sn = np.sin(sinus).astype(np.float32).T
    cs = np.cos(sinus).astype(np.float32).T
    out = np.zeros((128, len(idx)), np.float32)
    out[0:32] = cs
    out[32:64] = cs
    out[64:96] = sn
    out[96:128] = sn
    return out


def _in_maps(x, wq, wk, wv, wo, fw, fs):
    ident = np.eye(128, dtype=np.float32).astype(BFNP)
    mval = np.float32(NEGV / 16.0)
    tri = np.where(np.arange(128)[:, None] >= np.arange(128)[None, :],
                   np.float32(0.0), mval).astype(BFNP)
    mb = [np.full((128, 128), mval, np.float32).astype(BFNP),
          np.zeros((128, 128), np.float32).astype(BFNP)]
    maps, myidx_all = [], []
    for c in range(8):
        b, par = c // 2, c % 2
        myidx = np.concatenate([np.arange((2 * jj + par) * 128, (2 * jj + par + 1) * 128)
                                for jj in range(8)])
        pidx = np.concatenate([np.arange((2 * jj + 1 - par) * 128, (2 * jj + 2 - par) * 128)
                               for jj in range(8)])
        allidx = np.concatenate([myidx, pidx])
        myidx_all.append(myidx)
        maps.append({
            "xT": np.ascontiguousarray(x[b].T[:, allidx]),
            "xres": np.ascontiguousarray(x[b][myidx]),
            "wq": wq, "wk": wk, "wv": wv, "wo": wo, "fw": fw, "fs": fs,
            "tabq": _tables(myidx).astype(BFNP), "tabk": _tables(allidx).astype(BFNP),
            "maska": tri, "maskb": mb[par], "ident": ident,
        })
    return maps, myidx_all


def kernel(L0_attn_hidden_states, L0_attn_Wq, L0_attn_Wk, L0_attn_Wv, L0_attn_Wo,
           L0_ffn_kernel, L0_ffn_scale):
    x = np.asarray(L0_attn_hidden_states)
    perm = _rot_perm()
    wq = np.ascontiguousarray(np.asarray(L0_attn_Wq)[:, perm])
    wk = np.ascontiguousarray(np.asarray(L0_attn_Wk)[:, perm])
    wv = np.ascontiguousarray(np.asarray(L0_attn_Wv))
    wo = np.ascontiguousarray(np.asarray(L0_attn_Wo))
    fw = np.ascontiguousarray(np.asarray(L0_ffn_kernel).view(np.uint8))
    fs = np.asarray(L0_ffn_scale, dtype=np.float32).reshape(1, 1)

    nc = build_nc()
    in_maps, myidx_all = _in_maps(x, wq, wk, wv, wo, fw, fs)
    res = run_bass_kernel_spmd(nc, in_maps, core_ids=list(range(8)))
    out = np.empty((4, S, D), np.float32)
    for c in range(8):
        out[c // 2, myidx_all[c], :] = res.results[c]["out"]
    return out
